# version 19
# baseline (speedup 1.0000x reference)
import numpy as np
from contextlib import ExitStack

import concourse.bass as bass
import concourse.mybir as mybir
from concourse.bass_utils import run_bass_kernel_spmd

F32 = mybir.dt.float32
BF16 = mybir.dt.bfloat16
I32 = mybir.dt.int32
AF = mybir.ActivationFunctionType
ALU = mybir.AluOpType
AX = mybir.AxisListType

D = 1024
DF = D // 128
H = 4096
HF = H // 128
T = 512
EPS = 1e-5
INVD = 1.0 / D

VC_BDELTA, VC_A1, VC_B1 = 0, 8, 16
VC_W1S, VC_W2S, VC_H1S, VC_H2S, VC_W3S = 24, 56, 88, 120, 152
VC_D, VC_CA, VC_CB, VC_CC = 160, 161, 162, 163
NV = 164
AC_U_OWN, AC_C_OWN, AC_S_OWN, AC_U_RED, AC_C_RED, AC_S_RED = 0, 8, 16, 24, 32, 40

ENGS = ("sync", "tensor", "vector", "scalar", "gpsimd")

_DEBUG = False
LAST = None


class Ev:
    __slots__ = ("sem", "val", "eng", "dk")

    def __init__(self, sem, val, eng, dk):
        self.sem, self.val, self.eng, self.dk = sem, val, eng, dk


class Sched:
    def __init__(self, counts=None):
        self.q = {e: [] for e in ENGS}
        self.counts = dict(counts or {})
        self.watermark = {e: {} for e in ENGS}

    def op(self, eng, fn, waits=(), sig=None, dk=1):
        ws = []
        wm = self.watermark[eng]
        for ev in waits:
            if ev is None:
                continue
            if ev.eng == eng and ev.dk == 1:
                continue
            if wm.get(ev.sem, 0) >= ev.val:
                continue
            wm[ev.sem] = ev.val
            ws.append((ev.sem, ev.val))
        out = None
        if sig is not None:
            self.counts[sig] = self.counts.get(sig, 0) + dk
            out = Ev(sig, self.counts[sig], eng, dk)
            if dk == 1:
                wm[sig] = self.counts[sig]
        self.q[eng].append((ws, fn, sig, dk))
        return out

    def check(self, base=None):
        pc = {e: 0 for e in ENGS}
        cnt = dict(base or {})
        progress = True
        while progress:
            progress = False
            for e in ENGS:
                while pc[e] < len(self.q[e]):
                    ws, _, sig, dk = self.q[e][pc[e]]
                    if any(cnt.get(s, 0) < v for s, v in ws):
                        break
                    if sig:
                        cnt[sig] = cnt.get(sig, 0) + dk
                    pc[e] += 1
                    progress = True
        stuck = {e: (len(self.q[e]) - pc[e]) for e in ENGS if pc[e] < len(self.q[e])}
        if stuck:
            det = {e: [(s, v, cnt.get(s, 0)) for s, v in self.q[e][pc[e]][0]] for e in stuck}
            raise RuntimeError(f"schedule deadlock remaining={stuck} blocked_on={det}")
        return cnt

    def replay(self, eng, handle, sems):
        for ws, fn, sig, dk in self.q[eng]:
            for s, v in ws:
                handle.wait_ge(sems[s], v)
            instr = fn(handle)
            if sig is not None and instr is not None:
                instr.then_inc(sems[sig], dk)


class Res:

    def __init__(self):
        self.w = None
        self.r = []

    def wr(self):
        return list(self.r) + ([self.w] if self.w else [])

    def note_w(self, ev):
        self.w = ev
        self.r = []

    def note_r(self, ev):
        if ev is not None:
            self.r.append(ev)


def _tile_weight(w):
    O, I = w.shape
    return np.ascontiguousarray(w.reshape(O // 128, 128, I // 128, 128).transpose(0, 3, 2, 1))


def _cols(vec):
    n = vec.shape[0] // 128
    return np.ascontiguousarray(vec.reshape(n, 128).T)


def build_program(Q, debug=False):
    NCH = Q // T
    nc = bass.Bass("TRN2", target_bir_lowering=False, detect_race_conditions=False)

    ap = lambda t: t.ap()
    x_own = ap(nc.dram_tensor("x_own", [D, Q], F32, kind="ExternalInput"))
    x_red = ap(nc.dram_tensor("x_red", [D, Q], F32, kind="ExternalInput"))
    wproj32 = ap(nc.dram_tensor("wproj32", [3 * DF, 128, DF, 128], F32, kind="ExternalInput"))
    wdelta32 = ap(nc.dram_tensor("wdelta32", [DF, 128, DF, 128], F32, kind="ExternalInput"))
    w1_32 = ap(nc.dram_tensor("w1_32", [HF, 128, DF, 128], F32, kind="ExternalInput"))
    w2_32 = ap(nc.dram_tensor("w2_32", [HF, 128, DF, 128], F32, kind="ExternalInput"))
    h1_32 = ap(nc.dram_tensor("h1_32", [HF, 128, HF, 128], F32, kind="ExternalInput"))
    h2_32 = ap(nc.dram_tensor("h2_32", [HF, 128, HF, 128], F32, kind="ExternalInput"))
    w3_32 = ap(nc.dram_tensor("w3_32", [DF, 128, HF, 128], F32, kind="ExternalInput"))
    vecs_in = ap(nc.dram_tensor("vecs_in", [128, NV], F32, kind="ExternalInput"))
    meta_in = ap(nc.dram_tensor("meta_in", [1, 2], I32, kind="ExternalInput"))

    out_t = ap(nc.dram_tensor("out", [D, Q], F32, kind="ExternalOutput"))

    spill_kind = "ExternalOutput" if debug else "Internal"
    wb_names = ("wproj", "wdelta", "w1", "w2", "h1", "h2", "w3")
    wb_src = (wproj32, wdelta32, w1_32, w2_32, h1_32, h2_32, w3_32)
    wb_shapes = ([3 * DF, 128, DF, 128], [DF, 128, DF, 128], [HF, 128, DF, 128],
                 [HF, 128, DF, 128], [HF, 128, HF, 128], [HF, 128, HF, 128],
                 [DF, 128, HF, 128])
    wb = {nm: ap(nc.dram_tensor(nm + "_b", sh, BF16, kind="Internal"))
          for nm, sh in zip(wb_names, wb_shapes)}
    sp_dA = ap(nc.dram_tensor("sp_dA", [D, Q], F32, kind=spill_kind))
    sp_dB = ap(nc.dram_tensor("sp_dB", [D, Q], F32, kind=spill_kind))
    sp_res = ap(nc.dram_tensor("sp_res", [D, Q], F32, kind=spill_kind))

    xchg = ap(nc.dram_tensor("xchg", [2, 128, 48], F32, kind="Internal", addr_space="Shared"))
    flg = ap(nc.dram_tensor("flg", [2, 16], I32, kind="Internal", addr_space="Shared"))
    dbg = {}
    if debug:
        dbg["acc"] = ap(nc.dram_tensor("dbg_acc", [128, 48], F32, kind="ExternalOutput"))
        dbg["ucc"] = ap(nc.dram_tensor("dbg_ucc", [128, 32], F32, kind="ExternalOutput"))

    def dram_fm(t, c):
        return t[:, c * T:(c + 1) * T].rearrange("(f p) t -> p f t", p=128)

    def sb3(t):
        return t.ap().rearrange("p (f t) -> p f t", t=T)

    with ExitStack() as top:
        e = top.enter_context
        vecs = e(nc.sbuf_tensor("vecs", [128, NV], F32))
        acc = e(nc.sbuf_tensor("acc", [128, 48], F32))
        pacc = e(nc.sbuf_tensor("pacc", [128, 48], F32))
        ucc = e(nc.sbuf_tensor("ucc", [128, 32], F32))
        ones_cb = e(nc.sbuf_tensor("ones_cb", [128, 1], BF16))
        ones_rf = e(nc.sbuf_tensor("ones_rf", [1, 128], F32))
        meta_s = e(nc.sbuf_tensor("meta_s", [1, 2], I32))
        flg_s = e(nc.sbuf_tensor("flg_s", [1, 16], I32))
        flg_r = e(nc.sbuf_tensor("flg_r", [1, 16], I32))
        t1_row = [e(nc.sbuf_tensor(f"t1_row{s}", [1, T], F32)) for s in range(2)]
        var_row = [e(nc.sbuf_tensor(f"var_row{s}", [1, T], F32)) for s in range(2)]
        r_row = [e(nc.sbuf_tensor(f"r_row{s}", [1, T], F32)) for s in range(2)]
        mr_row = [e(nc.sbuf_tensor(f"mr_row{s}", [1, T], F32)) for s in range(2)]
        col_t = e(nc.sbuf_tensor("col_t", [128, 1], F32))
        dB_sl = [(e(nc.sbuf_tensor(f"db{i}", [128, T], F32)), Res()) for i in range(2)]
        ub = e(nc.sbuf_tensor("ub", [128, 128], F32))
        cb = e(nc.sbuf_tensor("cb", [128, 128], F32))
        sb_red = e(nc.sbuf_tensor("sb_red", [128, 64], F32))
        sb_own = e(nc.sbuf_tensor("sb_own", [128, 64], F32))
        ccs_buf = e(nc.sbuf_tensor("ccs_buf", [128, 64], F32))

        ps_g = [e(nc.psum_tensor(f"psg{i}", [128, T], F32)) for i in range(4)]
        ps_stat = [e(nc.psum_tensor(f"ps_stat{s}", [128, T], F32)) for s in range(2)]
        ps_mr = [e(nc.psum_tensor(f"ps_mr{s}", [128, T], F32)) for s in range(2)]

        sems = {}
        for nm in ("s_ld s_x g_cast s_wat t_mm t_stats t_bc a_ev a_row v_misc v_row v_fin "
                   "v_ap v_uc v_dl v_dA g_sp g_wr g_fl g_po g_rd v_cmb s2_ld w_ld "
                   "a2_ev v2_a g_out dbgst v_acc g_sq").split():
            sems[nm] = e(nc.semaphore(nm))

        r_bank = [Res() for _ in range(4)]
        rA = [Res(), Res()]
        rB = [Res(), Res()]
        rt1 = [Res(), Res()]
        rvr = [Res(), Res()]
        rrr = [Res(), Res()]
        rmr = [Res(), Res()]

        glb = {"li": 0}

        def ln_front(B, cast_sl, sq_sl, src_f, src_evs, vsig, asig,
                     cast_eng="vector"):
            s = glb["li"] % 2
            glb["li"] += 1
            stat_last = []
            last_cast = None
            for f in range(DF):
                cs, cr = cast_sl[f % 2]
                if cast_eng == "vector":
                    ev = B.op("vector",
                              (lambda f=f, cs=cs: lambda v: v.tensor_copy(cs[:, :], src_f(f)))(),
                              waits=cr.wr() + (list(src_evs) if f == 0 else []), sig=vsig)
                else:
                    ev = B.op("scalar",
                              (lambda f=f, cs=cs: lambda s_: s_.activation(cs[:, :], src_f(f), AF.Copy))(),
                              waits=cr.wr() + (list(src_evs) if f == 0 else []), sig=asig)
                last_cast = ev
                cr.note_w(ev)
                qs, qr = sq_sl[f % 2]
                ev2 = B.op("scalar",
                           (lambda cs=cs, qs=qs: lambda s_: s_.activation(qs[:, :], cs[:, :], AF.Square))(),
                           waits=[ev] + qr.wr(), sig=asig)
                cr.note_r(ev2)
                qr.note_w(ev2)
                mm1 = B.op("tensor",
                           (lambda f=f, cs=cs, s=s: lambda t: t.matmul(
                               ps_stat[s][0:1, :], ones_cb[:, 0:1], cs[:, :],
                               start=(f == 0), stop=(f == DF - 1)))(),
                           waits=[ev] + (rA[s].wr() + [glb.get("ones")] if f == 0 else []), sig="t_stats")
                cr.note_r(mm1)
                if f == 0:
                    rA[s].note_w(mm1)
                mm2 = B.op("tensor",
                           (lambda f=f, qs=qs, s=s: lambda t: t.matmul(
                               ps_stat[s][32:33, :], ones_cb[:, 0:1], qs[:, :],
                               start=(f == 0), stop=(f == DF - 1)))(),
                           waits=[ev2], sig="t_stats")
                qr.note_r(mm2)
                if f == DF - 1:
                    stat_last = [mm1, mm2]
            ev = B.op("vector", (lambda s=s: lambda v: v.tensor_scalar(
                t1_row[s][:, :], ps_stat[s][0:1, :], INVD, None, ALU.mult))(),
                waits=stat_last + rt1[s].wr(), sig=vsig)
            rA[s].note_r(ev)
            rt1[s].note_w(ev)
            B.op("vector", (lambda s=s: lambda v: v.tensor_mul(
                mr_row[s][:, :], t1_row[s][:, :], t1_row[s][:, :]))(), waits=rmr[s].wr())
            evq = B.op("vector", (lambda s=s: lambda v: v.tensor_scalar(
                var_row[s][:, :], ps_stat[s][32:33, :], INVD, EPS, ALU.mult, ALU.add))(),
                waits=rvr[s].wr(), sig=vsig)
            rA[s].note_r(evq)
            ev_v = B.op("vector", (lambda s=s: lambda v: v.tensor_sub(
                var_row[s][:, :], var_row[s][:, :], mr_row[s][:, :]))(), sig="v_row")
            rvr[s].note_w(ev_v)
            ev_l = B.op("scalar", (lambda s=s: lambda s_: s_.activation(
                r_row[s][:, :], var_row[s][:, :], AF.Ln))(),
                waits=[ev_v] + rrr[s].wr(), sig=asig)
            rvr[s].note_r(ev_l)
            ev_r = B.op("scalar", (lambda s=s: lambda s_: s_.activation(
                r_row[s][:, :], r_row[s][:, :], AF.Exp, scale=-0.5))(), sig="a_row")
            rrr[s].note_w(ev_r)
            ev_m = B.op("vector", (lambda s=s: lambda v: v.tensor_mul(
                mr_row[s][:, :], t1_row[s][:, :], r_row[s][:, :]))(),
                waits=[ev_r], sig="v_fin")
            rrr[s].note_r(ev_m)
            rt1[s].note_r(ev_m)
            rmr[s].note_w(ev_m)
            return dict(s=s, ev_m=ev_m, last_cast=last_cast)

        def ln_back(B, fr, apply_fn, apply_sig):
            s = fr["s"]
            bc1 = B.op("tensor", (lambda s=s: lambda t: t.matmul(
                ps_stat[s][:, :], ones_rf[0:1, :], r_row[s][0:1, :], start=True, stop=True))(),
                waits=[fr["ev_m"], glb.get("ones")] + rA[s].wr(), sig="t_bc")
            rrr[s].note_r(bc1)
            rA[s].note_w(bc1)
            bc2 = B.op("tensor", (lambda s=s: lambda t: t.matmul(
                ps_mr[s][:, :], ones_rf[0:1, :], mr_row[s][0:1, :], start=True, stop=True))(),
                waits=rB[s].wr(), sig="t_bc")
            rmr[s].note_r(bc2)
            rB[s].note_w(bc2)
            rb_ap, mb_ap = ps_stat[s], ps_mr[s]
            last = None
            for f in range(DF):
                last = apply_fn(f, [bc1, bc2] if f == 0 else [],
                                apply_sig if f == DF - 1 else None, rb_ap, mb_ap)
            rA[s].note_r(last)
            rB[s].note_r(last)
            return last

        def emit_ln(B, cast_sl, sq_sl, src_f, src_evs, apply_fn, apply_sig,
                    vsig, asig, cast_eng="vector"):
            fr = ln_front(B, cast_sl, sq_sl, src_f, src_evs, vsig, asig, cast_eng)
            return ln_back(B, fr, apply_fn, apply_sig)

        B = Sched()
        with ExitStack() as p1:
            e1 = p1.enter_context
            xb = [e1(nc.sbuf_tensor(f"xb{i}", [128, DF * T], F32)) for i in range(2)]
            r_xb = [Res(), Res()]
            cast_sl = [(e1(nc.sbuf_tensor(f"cb{i}", [128, T], BF16)), Res()) for i in range(2)]
            sq_sl = [(e1(nc.sbuf_tensor(f"qb{i}", [128, T], BF16)), Res()) for i in range(2)]
            xtmp = [e1(nc.sbuf_tensor(f"xt{i}", [128, T], F32)) for i in range(2)]
            xnbf = e1(nc.sbuf_tensor("xnbf", [128, DF * T], BF16))
            dpre = e1(nc.sbuf_tensor("dpre", [128, DF * T], BF16))
            cs_sl = [(e1(nc.sbuf_tensor(f"csl{i}", [128, T], F32)), Res()) for i in range(2)]
            e_sl = [(e1(nc.sbuf_tensor(f"esl{i}", [128, T], F32)), Res()) for i in range(2)]
            eA_sl = [(e1(nc.sbuf_tensor(f"easl{i}", [128, T], F32)), Res()) for i in range(2)]
            s_sl = [e1(nc.sbuf_tensor(f"ssl{i}", [128, T], F32)) for i in range(2)]
            delta_sl = [(e1(nc.sbuf_tensor(f"dl{i}", [128, T], F32)), Res()) for i in range(2)]
            dA_sl = [(e1(nc.sbuf_tensor(f"da{i}", [128, T], F32)), Res()) for i in range(2)]
            res_sl = [(e1(nc.sbuf_tensor(f"re{i}", [128, T], F32)), Res()) for i in range(2)]
            junk = e1(nc.sbuf_tensor("junk", [128, T], F32))
            watn = e1(nc.sbuf_tensor("watn", [128, 4 * DF * DF * 128], BF16))

            r_xnbf, r_dpre = Res(), Res()

            def watile(m, k):
                o = (m * DF + k) * 128
                return watn[:, o:o + 128]

            ev_vec = B.op("sync", lambda s: s.dma_start(out=vecs[:, :], in_=vecs_in), sig="s_ld", dk=16)
            B.op("sync", lambda s: s.dma_start(out=meta_s[:, :], in_=meta_in), sig="s_ld", dk=16)
            B.op("vector", lambda v: v.memset(ones_cb[:, :], 1.0), sig="v_misc")
            B.op("vector", lambda v: v.memset(ones_rf[:, :], 1.0), sig="v_misc")
            B.op("vector", lambda v: v.memset(acc[:, :], 0.0), sig="v_misc")
            B.op("vector", lambda v: v.memset(ub[:, :], 0.0), sig="v_misc")
            B.op("vector", lambda v: v.memset(cb[:, :], 0.0), sig="v_misc")
            B.op("vector", lambda v: v.memset(sb_red[:, :], 0.0), sig="v_misc")
            ev_init = B.op("vector", lambda v: v.memset(sb_own[:, :], 0.0), sig="v_misc")
            glb["ones"] = ev_init

            cast_evs = {}

            def emit_cast_piece(nm, mlo, mhi):
                srcw = dict(zip(wb_names, wb_src))[nm]
                cast_evs[nm] = B.op("gpsimd",
                                    (lambda srcw=srcw, dst=wb[nm], mlo=mlo, mhi=mhi:
                                     lambda g: g.dma_start(out=dst[mlo:mhi], in_=srcw[mlo:mhi]))(),
                                    sig="g_cast", dk=16)

            emit_cast_piece("wproj", 0, 3 * DF)
            emit_cast_piece("wdelta", 0, DF)
            cast_pieces = ([("w1", 16 * i, 16 * i + 16) for i in range(2)]
                           + [("w2", 16 * i, 16 * i + 16) for i in range(2)]
                           + [("h1", 4 * i, 4 * i + 4) for i in range(8)]
                           + [("h2", 4 * i, 4 * i + 4) for i in range(8)]
                           + [("w3", 0, DF)])
            ev_wat1 = B.op("scalar", lambda s: s.dma_start(
                out=watn.ap()[:, 0:3 * DF * DF * 128].rearrange("p (m k j) -> p m k j", k=DF, j=128),
                in_=wb["wproj"].rearrange("m p k j -> p m k j")),
                waits=[cast_evs["wproj"]], sig="s_wat", dk=16)
            ev_wat2 = B.op("scalar", lambda s: s.dma_start(
                out=watn.ap()[:, 3 * DF * DF * 128:].rearrange("p (m k j) -> p m k j", k=DF, j=128),
                in_=wb["wdelta"].rearrange("m p k j -> p m k j")),
                waits=[cast_evs["wdelta"]], sig="s_wat", dk=16)

            sched = [("own", c) for c in range(NCH)] + [("red", c) for c in range(NCH)]
            for g, (kind, c) in enumerate(sched):
                for _ in range(2):
                    if cast_pieces:
                        emit_cast_piece(*cast_pieces.pop(0))
                src_x = x_own if kind == "own" else x_red
                xcur, rx = xb[g % 2], r_xb[g % 2]
                ev_x = B.op("sync", (lambda src_x=src_x, c=c, xcur=xcur: lambda s: s.dma_start(
                    out=sb3(xcur), in_=dram_fm(src_x, c)))(),
                    waits=rx.wr(), sig="s_x", dk=16)
                rx.note_w(ev_x)

                def apply_ln1(f, w0, sig, rb, mb, xcur=xcur, rx=rx):
                    B.op("vector", (lambda f=f, xcur=xcur, rb=rb: lambda v: v.tensor_mul(
                        xtmp[f % 2][:, :], xcur[:, f * T:(f + 1) * T], rb[:, :]))(),
                        waits=w0 + (r_xnbf.wr() if f == 0 else []))
                    ev2 = B.op("vector", (lambda f=f, mb=mb: lambda v: v.tensor_sub(
                        xnbf[:, f * T:(f + 1) * T], xtmp[f % 2][:, :], mb[:, :]))(), sig=sig)
                    return ev2

                ev_ap = emit_ln(B, cast_sl, sq_sl,
                                (lambda xcur=xcur: lambda f: xcur[:, f * T:(f + 1) * T])(),
                                [ev_x], apply_ln1, "v_ap", "v_misc", "a_ev",
                                cast_eng="scalar")
                rx.note_r(ev_ap)
                r_xnbf.note_w(ev_ap)

                n_m = 24 if kind == "own" else 16
                ev_dpre_done = None
                ev_res_done = None
                for m in range(n_m):
                    bank, rb = ps_g[m % 2], r_bank[m % 2]
                    mm = None
                    for k in range(DF):
                        mm = B.op("tensor", (lambda m=m, k=k, bank=bank: lambda t: t.matmul(
                            bank[:, :], watile(m, k), xnbf[:, k * T:(k + 1) * T],
                            start=(k == 0), stop=(k == DF - 1)))(),
                            waits=(([ev_ap, ev_wat1, ev_wat2] + rb.wr()) if k == 0 else []),
                            sig=("t_mm" if k == DF - 1 else None))
                    rb.note_w(mm)
                    r_xnbf.note_r(mm)
                    if m < 8:
                        evv = B.op("scalar", (lambda m=m, bank=bank: lambda s: s.activation(
                            dpre[:, m * T:(m + 1) * T], bank[:, :], AF.Copy))(),
                            waits=[mm] + (r_dpre.wr() if m == 0 else []), sig="a_ev")
                        r_dpre.note_w(evv)
                        rb.note_r(evv)
                        if m == 7:
                            ev_dpre_done = evv
                    elif m < 16:
                        cs, cr = cs_sl[m % 2]
                        evv = B.op("scalar", (lambda cs=cs, bank=bank: lambda s: s.activation(
                            cs[:, :], bank[:, :], AF.Copy))(),
                            waits=[mm] + cr.wr(), sig="a_ev")
                        cr.note_w(evv)
                        rb.note_r(evv)
                        base = 0 if kind == "own" else 64
                        f = m - 8
                        pc = base + c * 8 + f
                        B.op("vector", (lambda cs=cs, f=f, pc=pc: lambda v: v.scalar_tensor_tensor(
                            junk[:, :], cs[:, :], 1.0, xnbf[:, f * T:(f + 1) * T],
                            ALU.bypass, ALU.mult, accum_out=ub[:, pc:pc + 1]))(),
                            waits=[evv])
                        e_u2 = B.op("vector", (lambda cs=cs, pc=pc: lambda v: v.reduce_sum(
                            cb[:, pc:pc + 1], cs[:, :], axis=AX.X))(), sig="v_uc")
                        cr.note_r(e_u2)
                    else:
                        f = m - 16
                        rs, rr = res_sl[f % 2]
                        evv = B.op("scalar", (lambda rs=rs, bank=bank: lambda s: s.activation(
                            rs[:, :], bank[:, :], AF.Copy))(),
                            waits=[mm] + rr.wr(), sig="a_ev")
                        rr.note_w(evv)
                        rb.note_r(evv)
                        ev_sp = B.op("gpsimd", (lambda rs=rs, f=f, c=c: lambda g_: g_.dma_start(
                            out=sp_res[f * 128:(f + 1) * 128, c * T:(c + 1) * T], in_=rs[:, :]))(),
                            waits=[evv], sig="g_sp", dk=16)
                        rr.note_r(ev_sp)

                ev_sc = None
                ev_da = None
                for m in range(DF):
                    bank, rb = ps_g[m % 2], r_bank[m % 2]
                    mm = None
                    for k in range(DF):
                        mm = B.op("tensor", (lambda m=m, k=k, bank=bank: lambda t: t.matmul(
                            bank[:, :], watile(24 + m, k), dpre[:, k * T:(k + 1) * T],
                            start=(k == 0), stop=(k == DF - 1)))(),
                            waits=(([ev_dpre_done] + rb.wr()) if k == 0 else []),
                            sig=("t_mm" if k == DF - 1 else None))
                    rb.note_w(mm)
                    r_dpre.note_r(mm)
                    es_, er_ = e_sl[m % 2]
                    ev_u = B.op("scalar", (lambda es_=es_, bank=bank, m=m: lambda s: s.activation(
                        es_[:, :], bank[:, :], AF.Exp,
                        bias=vecs[:, VC_BDELTA + m:VC_BDELTA + m + 1]))(),
                        waits=[mm, ev_vec] + er_.wr(), sig="a_ev")
                    er_.note_w(ev_u)
                    rb.note_r(ev_u)
                    ds, dr = delta_sl[m % 2]
                    ev_dl = B.op("scalar", (lambda es_=es_, ds=ds, m=m: lambda s: s.activation(
                        ds[:, :], es_[:, :], AF.Ln, bias=1.0))(),
                        waits=dr.wr(), sig="v_dl")
                    er_.note_r(ev_dl)
                    dr.note_w(ev_dl)
                    ev_s = B.op("vector", (lambda ds=ds, m=m: lambda v: v.tensor_scalar_mul(
                        s_sl[m % 2][:, :], ds[:, :],
                        vecs[:, VC_B1 + m:VC_B1 + m + 1]))(), waits=[ev_vec, ev_dl], sig="v_misc")
                    dr.note_r(ev_s)
                    if kind == "own":
                        bs, br_ = dB_sl[m % 2]
                        ev_sc = B.op("vector", (lambda bs=bs, m=m: lambda v: v.tensor_tensor_scan(
                            bs[:, :], s_sl[m % 2][:, :], s_sl[m % 2][:, :],
                            0.0, ALU.add, ALU.bypass))(),
                            waits=br_.wr(), sig="v_misc")
                        br_.note_w(ev_sc)
                        pc = c * 8 + m
                        B.op("vector", (lambda m=m, pc=pc: lambda v: v.reduce_sum(
                            sb_own[:, pc:pc + 1], s_sl[m % 2][:, :], axis=AX.X))())
                        ev_spb = B.op("gpsimd", (lambda bs=bs, m=m, c=c: lambda g_: g_.dma_start(
                            out=sp_dB[m * 128:(m + 1) * 128, c * T:(c + 1) * T], in_=bs[:, :]))(),
                            waits=[ev_sc], sig="g_sp", dk=16)
                        br_.note_r(ev_spb)
                        es, er = eA_sl[m % 2]
                        ev_e = B.op("scalar", (lambda es=es, ds=ds, m=m: lambda s: s.activation(
                            es[:, :], ds[:, :], AF.Exp,
                            scale=vecs[:, VC_A1 + m:VC_A1 + m + 1]))(),
                            waits=[ev_dl] + er.wr(), sig="a_ev")
                        er.note_w(ev_e)
                        dr.note_r(ev_e)
                        as_, ar_ = dA_sl[m % 2]
                        ev_da = B.op("vector", (lambda es=es, as_=as_, m=m: lambda v: v.tensor_scalar_mul(
                            as_[:, :], es[:, :],
                            vecs[:, VC_B1 + m:VC_B1 + m + 1]))(),
                            waits=[ev_e] + ar_.wr(), sig="v_dA")
                        er.note_r(ev_da)
                        ar_.note_w(ev_da)
                        ev_spa = B.op("gpsimd", (lambda as_=as_, m=m, c=c: lambda g_: g_.dma_start(
                            out=sp_dA[m * 128:(m + 1) * 128, c * T:(c + 1) * T], in_=as_[:, :]))(),
                            waits=[ev_da], sig="g_sp", dk=16)
                        ar_.note_r(ev_spa)
                    else:
                        pc = c * 8 + m
                        B.op("vector", (lambda m=m, pc=pc: lambda v: v.reduce_sum(
                            sb_red[:, pc:pc + 1], s_sl[m % 2][:, :], axis=AX.X))())


            while cast_pieces:
                emit_cast_piece(*cast_pieces.pop(0))

            B.check()
            with nc.Block("A") as blkA:
                blkA.sync(lambda h: B.replay("sync", h, sems))
                blkA.tensor(lambda h: B.replay("tensor", h, sems))
                blkA.vector(lambda h: B.replay("vector", h, sems))
                blkA.scalar(lambda h: B.replay("scalar", h, sems))
                blkA.gpsimd(lambda h: B.replay("gpsimd", h, sems))

            def _bva(v):
                tgts = []
                for m in range(DF):
                    tgts.append((AC_U_OWN + m, ub, m))
                    tgts.append((AC_C_OWN + m, cb, m))
                    tgts.append((AC_U_RED + m, ub, 64 + m))
                    tgts.append((AC_C_RED + m, cb, 64 + m))
                    tgts.append((AC_S_RED + m, sb_red, m))
                    tgts.append((AC_S_OWN + m, sb_own, m))
                for col, srcb, base in tgts:
                    v.tensor_add(acc[:, col:col + 1], srcb[:, base:base + 1], srcb[:, base + 8:base + 9])
                for cc_ in range(2, 8):
                    for col, srcb, base in tgts:
                        v.tensor_add(acc[:, col:col + 1], acc[:, col:col + 1],
                                     srcb[:, base + 8 * cc_:base + 8 * cc_ + 1])
                for _ in range(3):
                    v.nop(cycle_cnt=250)
                v.engine_nop().then_inc(sems["v_acc"], 1)
                _bv(v)

            def _bg(g_):
                g_.wait_ge(sems["v_acc"], 1)
                with g_.register("pos") as pos, g_.register("nonce") as nonce:
                    g_.load(pos, meta_s[0:1, 0:1])
                    g_.load(nonce, meta_s[0:1, 1:2])
                    g_.memset(flg_s[:, :], 0)
                    g_.store(flg_s[0:1, 0:1], nonce)
                    for p in range(2):
                        with g_.If_eq(pos, p):
                            g_.dma_start(out=xchg[p, :, :], in_=acc[:, :]).then_inc(sems["g_wr"], 16)
                    g_.wait_ge(sems["g_wr"], 16)
                    for p in range(2):
                        with g_.If_eq(pos, p):
                            g_.dma_start(out=flg[p:p + 1, :], in_=flg_s[0:1, :]).then_inc(sems["g_fl"], 16)
                    g_.wait_ge(sems["g_fl"], 16)
                    with g_.register("cond") as cond, g_.register("cnt") as cnt, g_.register("fv") as fv:
                        g_.reg_mov(cond, 1)
                        g_.reg_mov(cnt, 16)
                        with g_.While(cond):
                            for p in range(2):
                                with g_.If_eq(pos, p):
                                    g_.dma_start(out=flg_r[:, :], in_=flg[1 - p:2 - p, :]).then_inc(sems["g_po"], 16)
                            g_.wait_ge(sems["g_po"], cnt)
                            g_.reg_add(cnt, cnt, 16)
                            g_.load(fv, flg_r[0:1, 0:1])
                            g_.reg_alu(cond, fv, nonce, ALU.not_equal)
                    for p in range(2):
                        with g_.If_eq(pos, p):
                            g_.dma_start(out=pacc[:, :], in_=xchg[1 - p, :, :]).then_inc(sems["g_rd"], 16)
                    g_.wait_ge(sems["g_rd"], 16)

            def _bv(v):
                v.wait_ge(sems["g_rd"], 16)

                def space():
                    for _ in range(3):
                        v.nop(cycle_cnt=250)
                v.tensor_add(ucc[:, 0:8], acc[:, AC_U_OWN:AC_U_OWN + 8], acc[:, AC_U_RED:AC_U_RED + 8])
                v.tensor_add(ucc[:, 8:16], acc[:, AC_C_OWN:AC_C_OWN + 8], acc[:, AC_C_RED:AC_C_RED + 8])
                v.tensor_scalar_mul(ucc[:, 24:32], acc[:, AC_S_RED:AC_S_RED + 8], vecs[:, VC_CA:VC_CA + 1])
                space()
                v.tensor_add(ucc[:, 0:8], ucc[:, 0:8], pacc[:, AC_U_OWN:AC_U_OWN + 8])
                v.tensor_add(ucc[:, 8:16], ucc[:, 8:16], pacc[:, AC_C_OWN:AC_C_OWN + 8])
                v.scalar_tensor_tensor(ucc[:, 24:32], pacc[:, AC_S_OWN:AC_S_OWN + 8],
                                       vecs[:, VC_CB:VC_CB + 1], ucc[:, 24:32], ALU.mult, ALU.add)
                space()
                v.tensor_add(ucc[:, 0:8], ucc[:, 0:8], pacc[:, AC_U_RED:AC_U_RED + 8])
                v.tensor_add(ucc[:, 8:16], ucc[:, 8:16], pacc[:, AC_C_RED:AC_C_RED + 8])
                v.scalar_tensor_tensor(ucc[:, 24:32], pacc[:, AC_S_RED:AC_S_RED + 8],
                                       vecs[:, VC_CC:VC_CC + 1], ucc[:, 24:32], ALU.mult, ALU.add)
                space()
                for c_ in range(NCH):
                    v.tensor_mul(ccs_buf[:, c_ * 8:(c_ + 1) * 8], ucc[:, 24:32], ucc[:, 8:16])
                    v.tensor_add(ucc[:, 24:32], ucc[:, 24:32], sb_own[:, c_ * 8:(c_ + 1) * 8])
                    space()
                v.engine_nop().then_inc(sems["v_cmb"], 1)


        B2 = Sched(counts=B.counts)
        with ExitStack() as p3:
            e3 = p3.enter_context
            R1 = e3(nc.sbuf_tensor("R1", [128, DF * T], F32))
            R2 = [e3(nc.sbuf_tensor(f"R2_{i}", [128, DF * T], F32)) for i in range(2)]
            R3 = [e3(nc.sbuf_tensor(f"R3_{i}", [128, DF * T], F32)) for i in range(2)]
            R4 = [e3(nc.sbuf_tensor(f"R4_{i}", [128, DF * T], F32)) for i in range(2)]
            R5 = e3(nc.sbuf_tensor("R5", [128, DF * T], F32))
            R6 = e3(nc.sbuf_tensor("R6", [128, DF * T], BF16))
            cast2 = [(e3(nc.sbuf_tensor(f"c2_{i}", [128, T], BF16)), Res()) for i in range(2)]
            sq2 = [(e3(nc.sbuf_tensor(f"q2_{i}", [128, T], BF16)), Res()) for i in range(2)]
            xtmp2 = [e3(nc.sbuf_tensor(f"xt2_{i}", [128, T], F32)) for i in range(2)]
            gsl = [(e3(nc.sbuf_tensor(f"gg{i}", [128, T], F32)), Res()) for i in range(2)]
            wbuf = e3(nc.sbuf_tensor("wbuf", [128, 4 * HF * 128], BF16))
            r_wslot = [Res() for _ in range(4)]
            r_R1, r_R5, r_R6 = Res(), Res(), Res()
            r_R2 = [Res(), Res()]
            r_R3 = [Res(), Res()]
            r_R4 = [Res(), Res()]

            def wtile(s, k):
                o = s * HF * 128 + k * 128
                return wbuf[:, o:o + 128]

            R1b = R1.ap().bitcast(BF16)
            R2b = [R2[i].ap().bitcast(BF16) for i in range(2)]
            R3b = [R3[i].ap().bitcast(BF16) for i in range(2)]
            R4b = [R4[i].ap().bitcast(BF16) for i in range(2)]

            def a_tile(m, p):
                bb = R2b[p] if m < 16 else R3b[p]
                return bb[:, (m % 16) * T:((m % 16) + 1) * T]

            def b_tile(m, p):
                bb = R4b[p] if m < 16 else R1b
                return bb[:, (m % 16) * T:((m % 16) + 1) * T]

            c_tile = a_tile
            slot_i = 0

            def _bva_wrap(v):
                _bva(v)
                return None
            B2.counts["v_cmb"] = B2.counts.get("v_cmb", 0) + 1
            ev_cmb = Ev("v_cmb", B2.counts["v_cmb"], "vector", 1)
            B2.q["vector"].append(([], _bva_wrap, None, 1))
            B2.watermark["vector"]["v_cmb"] = B2.counts["v_cmb"]
            B2.q["gpsimd"].append(([], lambda g_: _bg(g_), None, 1))
            if debug:
                def _dbg_store(sy):
                    sy.wait_ge(sems["v_cmb"], 1)
                    sy.dma_start(out=dbg["acc"], in_=acc[:, :]).then_inc(sems["dbgst"], 16)
                    sy.dma_start(out=dbg["ucc"], in_=ucc[:, :]).then_inc(sems["dbgst"], 16)
                    sy.wait_ge(sems["dbgst"], 32)
                B2.q["sync"].append(([], _dbg_store, None, 1))

            def stream2(arr, kf, mlo, mhi, cast_ev=None):
                nonlocal slot_i
                s = slot_i % 4
                slot_i += 1
                ncols = (mhi - mlo) * kf * 128
                ev = B2.op("sync", (lambda arr=arr, mlo=mlo, mhi=mhi, s=s, ncols=ncols, kf=kf: lambda sy: sy.dma_start(
                    out=wbuf.ap()[:, s * HF * 128: s * HF * 128 + ncols].rearrange(
                        "p (m k j) -> p m k j", k=kf, j=128),
                    in_=arr[mlo:mhi].rearrange("m p k j -> p m k j")))(),
                    waits=r_wslot[s].wr() + [cast_ev], sig="w_ld", dk=16)
                r_wslot[s].note_w(ev)
                return ev, s

            def prologue(c):
                p = c % 2
                lv = {}
                for nm, R, rr, src in (("R2", R2[p], r_R2[p], sp_dA),
                                       ("R3", R3[p], r_R3[p], sp_dB),
                                       ("R4", R4[p], r_R4[p], sp_res)):
                    lv[nm] = B2.op("sync", (lambda R=R, src=src, c=c: lambda s: s.dma_start(
                        out=sb3(R), in_=dram_fm(src, c)))(),
                        waits=rr.wr(), sig="s2_ld", dk=16)
                    rr.note_w(lv[nm])
                ev_y = None
                for f in range(DF):
                    B2.op("vector", (lambda f=f, c=c, p=p: lambda v: v.tensor_scalar(
                        R4[p][:, f * T:(f + 1) * T], R4[p][:, f * T:(f + 1) * T],
                        vecs[:, VC_D:VC_D + 1], ccs_buf[:, c * 8 + f:c * 8 + f + 1],
                        ALU.mult, ALU.add))(),
                        waits=([lv["R4"], ev_cmb] if f == 0 else []))
                    B2.op("vector", (lambda f=f, p=p: lambda v: v.scalar_tensor_tensor(
                        R4[p][:, f * T:(f + 1) * T], R3[p][:, f * T:(f + 1) * T],
                        ucc[:, 8 + f:8 + f + 1], R4[p][:, f * T:(f + 1) * T], ALU.mult, ALU.add))(),
                        waits=[lv["R3"]] if f == 0 else [])
                    ev_y = B2.op("vector", (lambda f=f, p=p: lambda v: v.scalar_tensor_tensor(
                        R4[p][:, f * T:(f + 1) * T], R2[p][:, f * T:(f + 1) * T],
                        ucc[:, 0 + f:0 + f + 1], R4[p][:, f * T:(f + 1) * T], ALU.mult, ALU.add))(),
                        waits=[lv["R2"]] if f == 0 else [],
                        sig=("v2_a" if f == DF - 1 else None))
                r_R2[p].note_r(ev_y)
                r_R3[p].note_r(ev_y)
                r_R4[p].note_w(ev_y)
                fr_attn = ln_front(B2, cast2, sq2,
                                   (lambda p=p: lambda f: R4[p][:, f * T:(f + 1) * T])(),
                                   [ev_y], "v2_a", "a2_ev")
                return ev_y, fr_attn

            pro_y = prologue(0)
            for c in range(NCH):
                p = c % 2
                ev_y, fr_attn = pro_y
                lev_x = B2.op("sync", (lambda c=c: lambda s: s.dma_start(
                    out=sb3(R1), in_=dram_fm(x_own, c)))(),
                    waits=r_R1.wr(), sig="s2_ld", dk=16)
                r_R1.note_w(lev_x)

                def apply_attn(f, w0, sig, rb, mb):
                    B2.op("vector", (lambda f=f, p=p, rb=rb: lambda v: v.tensor_mul(
                        R5[:, f * T:(f + 1) * T], R4[p][:, f * T:(f + 1) * T], rb[:, :]))(),
                        waits=w0 + (r_R5.wr() if f == 0 else []))
                    B2.op("vector", (lambda f=f, mb=mb: lambda v: v.tensor_sub(
                        R5[:, f * T:(f + 1) * T], R5[:, f * T:(f + 1) * T], mb[:, :]))())
                    return B2.op("vector", (lambda f=f: lambda v: v.tensor_add(
                        R5[:, f * T:(f + 1) * T], R5[:, f * T:(f + 1) * T], R1[:, f * T:(f + 1) * T]))(),
                        waits=([lev_x] if f == 0 else []), sig=sig)

                ev_x1 = ln_back(B2, fr_attn, apply_attn, "v2_a")
                r_R4[p].note_r(ev_x1)
                r_R1.note_r(ev_x1)
                r_R5.note_w(ev_x1)

                def apply_ln2(f, w0, sig, rb, mb):
                    B2.op("vector", (lambda f=f, rb=rb: lambda v: v.tensor_mul(
                        xtmp2[f % 2][:, :], R5[:, f * T:(f + 1) * T], rb[:, :]))(),
                        waits=w0 + (r_R6.wr() if f == 0 else []))
                    return B2.op("vector", (lambda f=f, mb=mb: lambda v: v.tensor_sub(
                        R6[:, f * T:(f + 1) * T], xtmp2[f % 2][:, :], mb[:, :]))(), sig=sig)

                ev_xn1 = emit_ln(B2, cast2, sq2, lambda f: R5[:, f * T:(f + 1) * T],
                                 [ev_x1], apply_ln2, "v2_a", "v2_a", "a2_ev")
                r_R5.note_r(ev_xn1)
                r_R6.note_w(ev_xn1)

                ev_a_last = None
                for j in range(HF // 4):
                    e_w1, s1 = stream2(wb["w1"], DF, 4 * j, 4 * j + 4, cast_evs["w1"])
                    e_w2, s2 = stream2(wb["w2"], DF, 4 * j, 4 * j + 4, cast_evs["w2"])
                    for mi in range(4):
                        m = 4 * j + mi
                        b1, rb1 = ps_g[m % 2], r_bank[m % 2]
                        b2, rb2 = ps_g[2 + m % 2], r_bank[2 + m % 2]
                        mm1 = mm2 = None
                        for k in range(DF):
                            mm1 = B2.op("tensor", (lambda s1=s1, mi=mi, k=k, b1=b1: lambda t: t.matmul(
                                b1[:, :], wtile(s1, mi * DF + k), R6[:, k * T:(k + 1) * T],
                                start=(k == 0), stop=(k == DF - 1)))(),
                                waits=(([e_w1, ev_xn1] + rb1.wr()) if k == 0 else []),
                                sig=("t_mm" if k == DF - 1 else None))
                        rb1.note_w(mm1)
                        r_wslot[s1].note_r(mm1)
                        r_R6.note_r(mm1)
                        for k in range(DF):
                            mm2 = B2.op("tensor", (lambda s2=s2, mi=mi, k=k, b2=b2: lambda t: t.matmul(
                                b2[:, :], wtile(s2, mi * DF + k), R6[:, k * T:(k + 1) * T],
                                start=(k == 0), stop=(k == DF - 1)))(),
                                waits=(([e_w2] + rb2.wr()) if k == 0 else []),
                                sig=("t_mm" if k == DF - 1 else None))
                        rb2.note_w(mm2)
                        r_wslot[s2].note_r(mm2)
                        r_R6.note_r(mm2)
                        gs, gr = gsl[m % 2]
                        ev_g = B2.op("scalar", (lambda gs=gs, b1=b1, m=m: lambda s: s.activation(
                            gs[:, :], b1[:, :], AF.Silu,
                            scale=vecs[:, VC_W1S + m:VC_W1S + m + 1]))(),
                            waits=[mm1] + gr.wr(), sig="a2_ev")
                        gr.note_w(ev_g)
                        rb1.note_r(ev_g)
                        wx = [mm2, ev_g]
                        if m == 0:
                            wx += r_R2[p].wr()
                        if m == 16:
                            wx += r_R3[p].wr()
                        ev_a = B2.op("vector", (lambda gs=gs, b2=b2, m=m, p=p: lambda v: v.scalar_tensor_tensor(
                            a_tile(m, p), b2[:, :], vecs[:, VC_W2S + m:VC_W2S + m + 1],
                            gs[:, :], ALU.mult, ALU.mult))(),
                            waits=wx, sig="v2_a")
                        gr.note_r(ev_a)
                        rb2.note_r(ev_a)
                        if m == 0:
                            r_R2[p].note_w(ev_a)
                        if m == 16:
                            r_R3[p].note_w(ev_a)
                        ev_a_last = ev_a

                ev_b_last = None
                for m in range(HF):
                    e_w, s = stream2(wb["h1"], HF, m, m + 1, cast_evs["h1"])
                    bank, rb = ps_g[m % 4], r_bank[m % 4]
                    mm = None
                    for k in range(HF):
                        mm = B2.op("tensor", (lambda s=s, k=k, bank=bank, p=p: lambda t: t.matmul(
                            bank[:, :], wtile(s, k), a_tile(k, p),
                            start=(k == 0), stop=(k == HF - 1)))(),
                            waits=(([e_w, ev_a_last] + rb.wr()) if k == 0 else []),
                            sig=("t_mm" if k == HF - 1 else None))
                    rb.note_w(mm)
                    r_wslot[s].note_r(mm)
                    r_R2[p].note_r(mm)
                    r_R3[p].note_r(mm)
                    wx = [mm]
                    if m == 0:
                        wx += r_R4[p].wr()
                    if m == 16:
                        wx += r_R1.wr()
                    ev_b = B2.op("scalar", (lambda m=m, bank=bank, p=p: lambda s: s.activation(
                        b_tile(m, p), bank[:, :], AF.Gelu,
                        scale=vecs[:, VC_H1S + m:VC_H1S + m + 1]))(),
                        waits=wx, sig="a2_ev")
                    rb.note_r(ev_b)
                    if m == 0:
                        r_R4[p].note_w(ev_b)
                    if m == 16:
                        r_R1.note_w(ev_b)
                    ev_b_last = ev_b

                if c + 1 < NCH:
                    pro_y = prologue(c + 1)

                ev_c_last = None
                for m in range(HF):
                    e_w, s = stream2(wb["h2"], HF, m, m + 1, cast_evs["h2"])
                    bank, rb = ps_g[m % 4], r_bank[m % 4]
                    mm = None
                    for k in range(HF):
                        mm = B2.op("tensor", (lambda s=s, k=k, bank=bank, p=p: lambda t: t.matmul(
                            bank[:, :], wtile(s, k), b_tile(k, p),
                            start=(k == 0), stop=(k == HF - 1)))(),
                            waits=(([e_w, ev_b_last] + rb.wr()) if k == 0 else []),
                            sig=("t_mm" if k == HF - 1 else None))
                    rb.note_w(mm)
                    r_wslot[s].note_r(mm)
                    r_R4[p].note_r(mm)
                    r_R1.note_r(mm)
                    wx = [mm]
                    if m == 0:
                        wx += r_R2[p].wr()
                    if m == 16:
                        wx += r_R3[p].wr()
                    ev_c = B2.op("scalar", (lambda m=m, bank=bank, p=p: lambda s: s.activation(
                        c_tile(m, p), bank[:, :], AF.Gelu,
                        scale=vecs[:, VC_H2S + m:VC_H2S + m + 1]))(),
                        waits=wx, sig="a2_ev")
                    rb.note_r(ev_c)
                    if m == 0:
                        r_R2[p].note_w(ev_c)
                    if m == 16:
                        r_R3[p].note_w(ev_c)
                    ev_c_last = ev_c

                ev_f = None
                for m in range(DF):
                    e_w, s = stream2(wb["w3"], HF, m, m + 1, cast_evs["w3"])
                    bank, rb = ps_g[m % 4], r_bank[m % 4]
                    mm = None
                    for k in range(HF):
                        mm = B2.op("tensor", (lambda s=s, k=k, bank=bank, p=p: lambda t: t.matmul(
                            bank[:, :], wtile(s, k), c_tile(k, p),
                            start=(k == 0), stop=(k == HF - 1)))(),
                            waits=(([e_w, ev_c_last] + rb.wr()) if k == 0 else []),
                            sig=("t_mm" if k == HF - 1 else None))
                    rb.note_w(mm)
                    r_wslot[s].note_r(mm)
                    r_R2[p].note_r(mm)
                    r_R3[p].note_r(mm)
                    wx = [mm]
                    if m == 0:
                        wx += r_R4[p].wr()
                    ev_f = B2.op("scalar", (lambda m=m, bank=bank, p=p: lambda s: s.activation(
                        R4[p][:, m * T:(m + 1) * T], bank[:, :], AF.Copy,
                        scale=vecs[:, VC_W3S + m:VC_W3S + m + 1]))(),
                        waits=wx, sig="a2_ev")
                    rb.note_r(ev_f)
                    r_R4[p].note_w(ev_f)

                def apply_fl(f, w0, sig, rb, mb):
                    B2.op("vector", (lambda f=f, p=p, rb=rb: lambda v: v.tensor_mul(
                        R4[p][:, f * T:(f + 1) * T], R4[p][:, f * T:(f + 1) * T], rb[:, :]))(),
                        waits=w0)
                    B2.op("vector", (lambda f=f, p=p, mb=mb: lambda v: v.tensor_sub(
                        R4[p][:, f * T:(f + 1) * T], R4[p][:, f * T:(f + 1) * T], mb[:, :]))())
                    return B2.op("vector", (lambda f=f, p=p: lambda v: v.tensor_add(
                        R4[p][:, f * T:(f + 1) * T], R4[p][:, f * T:(f + 1) * T],
                        R5[:, f * T:(f + 1) * T]))(),
                        sig=sig)

                ev_out = emit_ln(B2, cast2, sq2,
                                 (lambda p=p: lambda f: R4[p][:, f * T:(f + 1) * T])(),
                                 [ev_f], apply_fl, "v2_a", "v2_a", "a2_ev")
                r_R4[p].note_w(ev_out)
                r_R5.note_r(ev_out)

                ev_st = B2.op("sync", (lambda c=c, p=p: lambda sy: sy.dma_start(
                    out=dram_fm(out_t, c), in_=sb3(R4[p])))(),
                    waits=[ev_out], sig="g_out", dk=16)
                r_R4[p].note_r(ev_st)

            gout_total = B2.counts.get("g_out", 0)
            B2.op("sync", (lambda gt=gout_total: lambda sy: sy.wait_ge(sems["g_out"], gt))())

            base_counts = dict(B.counts)
            base_counts["v_cmb"] = base_counts.get("v_cmb", 0) + 1
            B2.check(base=base_counts)
            with nc.Block("C") as blkC:
                blkC.sync(lambda h: B2.replay("sync", h, sems))
                blkC.tensor(lambda h: B2.replay("tensor", h, sems))
                blkC.vector(lambda h: B2.replay("vector", h, sems))
                blkC.scalar(lambda h: B2.replay("scalar", h, sems))
                blkC.gpsimd(lambda h: B2.replay("gpsimd", h, sems))

    return nc


def kernel(**inputs):
    global LAST
    x = np.asarray(inputs["L0_attn_x"])
    Wp = np.asarray(inputs["L0_attn_W_proj"])
    A = np.asarray(inputs["L0_attn_A"]).reshape(-1)
    Bp = np.asarray(inputs["L0_attn_Bp"]).reshape(-1)
    Dsc = np.asarray(inputs["L0_attn_D"]).reshape(-1)
    Wd = np.asarray(inputs["L0_attn_W_delta"])
    bd = np.asarray(inputs["L0_attn_b_delta"]).reshape(-1)
    w1w = np.asarray(inputs["L0_ffn_w1_w"]); w1s = np.asarray(inputs["L0_ffn_w1_s"])
    w2w = np.asarray(inputs["L0_ffn_w2_w"]); w2s = np.asarray(inputs["L0_ffn_w2_s"])
    h1w = np.asarray(inputs["L0_ffn_h1_w"]); h1s = np.asarray(inputs["L0_ffn_h1_s"])
    h2w = np.asarray(inputs["L0_ffn_h2_w"]); h2s = np.asarray(inputs["L0_ffn_h2_s"])
    w3w = np.asarray(inputs["L0_ffn_w3_w"]); w3s = np.asarray(inputs["L0_ffn_w3_s"])

    Bsz, L, _ = x.shape
    Q = L // 4
    nc = build_program(Q, debug=_DEBUG)

    wproj_used = np.concatenate([Wp[0:D], Wp[2 * D:3 * D], Wp[3 * D:4 * D]], axis=0)
    tw = {
        "wproj32": _tile_weight(wproj_used), "wdelta32": _tile_weight(Wd),
        "w1_32": _tile_weight(w1w), "w2_32": _tile_weight(w2w),
        "h1_32": _tile_weight(h1w), "h2_32": _tile_weight(h2w),
        "w3_32": _tile_weight(w3w),
    }
    vecs = np.zeros((128, NV), dtype=np.float32)
    vecs[:, VC_BDELTA:VC_BDELTA + DF] = _cols(bd)
    vecs[:, VC_A1:VC_A1 + DF] = _cols(A)
    vecs[:, VC_B1:VC_B1 + DF] = _cols(Bp)
    vecs[:, VC_W1S:VC_W1S + HF] = _cols(w1s)
    vecs[:, VC_W2S:VC_W2S + HF] = _cols(w2s)
    vecs[:, VC_H1S:VC_H1S + HF] = _cols(h1s)
    vecs[:, VC_H2S:VC_H2S + HF] = _cols(h2s)
    vecs[:, VC_W3S:VC_W3S + DF] = _cols(w3s)
    vecs[:, VC_D] = Dsc[0]

    CARRY_COEF = {0: (0., 0., 0.), 1: (0., 1., 0.), 2: (1., 0., 1.), 3: (1., 1., 1.)}
    RED_Q = {0: 2, 1: 3, 2: 0, 3: 1}

    rng = np.random.default_rng()
    nonces = [int(rng.integers(1, 2**31 - 1)) for _ in range(4)]
    in_maps = []
    for core in range(8):
        b, r = core // 4, core % 4
        xt = np.ascontiguousarray(x[b].T)
        vc = vecs.copy()
        a_, b_, c_ = CARRY_COEF[r]
        vc[:, VC_CA], vc[:, VC_CB], vc[:, VC_CC] = a_, b_, c_
        in_maps.append({
            "x_own": np.ascontiguousarray(xt[:, r * Q:(r + 1) * Q]),
            "x_red": np.ascontiguousarray(xt[:, RED_Q[r] * Q:(RED_Q[r] + 1) * Q]),
            "vecs_in": vc,
            "meta_in": np.array([[core % 2, nonces[core // 2]]], dtype=np.int32),
            **tw,
        })

    res = run_bass_kernel_spmd(nc, in_maps, core_ids=list(range(8)))
    LAST = res

    out = np.empty_like(x)
    for core in range(8):
        b, r = core // 4, core % 4
        out[b, r * Q:(r + 1) * Q, :] = res.results[core]["out"].T
    return out


# version 20
# speedup vs baseline: 2.5611x; 2.5611x over previous
import numpy as np
from contextlib import ExitStack

import concourse.bass as bass
import concourse.mybir as mybir
from concourse.bass_utils import run_bass_kernel_spmd

F32 = mybir.dt.float32
BF16 = mybir.dt.bfloat16
I32 = mybir.dt.int32
AF = mybir.ActivationFunctionType
ALU = mybir.AluOpType
AX = mybir.AxisListType

D = 1024
DF = D // 128
H = 4096
HF = H // 128
T = 512
EPS = 1e-5
INVD = 1.0 / D

VC_BDELTA, VC_A1, VC_B1 = 0, 8, 16
VC_W1S, VC_W2S, VC_H1S, VC_H2S, VC_W3S = 24, 56, 88, 120, 152
VC_D, VC_CA, VC_CB, VC_CC = 160, 161, 162, 163
NV = 164
AC_U_OWN, AC_C_OWN, AC_S_OWN, AC_U_RED, AC_C_RED, AC_S_RED = 0, 8, 16, 24, 32, 40

ENGS = ("sync", "tensor", "vector", "scalar", "gpsimd")

_DEBUG = False
LAST = None


class Ev:
    __slots__ = ("sem", "val", "eng", "dk")

    def __init__(self, sem, val, eng, dk):
        self.sem, self.val, self.eng, self.dk = sem, val, eng, dk


class Sched:
    def __init__(self, counts=None):
        self.q = {e: [] for e in ENGS}
        self.counts = dict(counts or {})
        self.watermark = {e: {} for e in ENGS}

    def op(self, eng, fn, waits=(), sig=None, dk=1):
        ws = []
        wm = self.watermark[eng]
        for ev in waits:
            if ev is None:
                continue
            if ev.eng == eng and ev.dk == 1:
                continue
            if wm.get(ev.sem, 0) >= ev.val:
                continue
            wm[ev.sem] = ev.val
            ws.append((ev.sem, ev.val))
        out = None
        if sig is not None:
            self.counts[sig] = self.counts.get(sig, 0) + dk
            out = Ev(sig, self.counts[sig], eng, dk)
            if dk == 1:
                wm[sig] = self.counts[sig]
        self.q[eng].append((ws, fn, sig, dk))
        return out

    def check(self, base=None):
        pc = {e: 0 for e in ENGS}
        cnt = dict(base or {})
        progress = True
        while progress:
            progress = False
            for e in ENGS:
                while pc[e] < len(self.q[e]):
                    ws, _, sig, dk = self.q[e][pc[e]]
                    if any(cnt.get(s, 0) < v for s, v in ws):
                        break
                    if sig:
                        cnt[sig] = cnt.get(sig, 0) + dk
                    pc[e] += 1
                    progress = True
        stuck = {e: (len(self.q[e]) - pc[e]) for e in ENGS if pc[e] < len(self.q[e])}
        if stuck:
            det = {e: [(s, v, cnt.get(s, 0)) for s, v in self.q[e][pc[e]][0]] for e in stuck}
            raise RuntimeError(f"schedule deadlock remaining={stuck} blocked_on={det}")
        return cnt

    def replay(self, eng, handle, sems):
        for ws, fn, sig, dk in self.q[eng]:
            for s, v in ws:
                handle.wait_ge(sems[s], v)
            instr = fn(handle)
            if sig is not None and instr is not None:
                instr.then_inc(sems[sig], dk)


class Res:

    def __init__(self):
        self.w = None
        self.r = []

    def wr(self):
        return list(self.r) + ([self.w] if self.w else [])

    def note_w(self, ev):
        self.w = ev
        self.r = []

    def note_r(self, ev):
        if ev is not None:
            self.r.append(ev)


def _tile_weight(w):
    O, I = w.shape
    return np.ascontiguousarray(w.reshape(O // 128, 128, I // 128, 128).transpose(0, 3, 2, 1))


def _cols(vec):
    n = vec.shape[0] // 128
    return np.ascontiguousarray(vec.reshape(n, 128).T)


def build_program(Q, debug=False):
    NCH = Q // T
    nc = bass.Bass("TRN2", target_bir_lowering=False, detect_race_conditions=False)

    ap = lambda t: t.ap()
    x_own = ap(nc.dram_tensor("x_own", [D, Q], F32, kind="ExternalInput"))
    x_red = ap(nc.dram_tensor("x_red", [D, Q], F32, kind="ExternalInput"))
    wproj32 = ap(nc.dram_tensor("wproj32", [3 * DF, 128, DF, 128], F32, kind="ExternalInput"))
    wdelta32 = ap(nc.dram_tensor("wdelta32", [DF, 128, DF, 128], F32, kind="ExternalInput"))
    w1_32 = ap(nc.dram_tensor("w1_32", [HF, 128, DF, 128], F32, kind="ExternalInput"))
    w2_32 = ap(nc.dram_tensor("w2_32", [HF, 128, DF, 128], F32, kind="ExternalInput"))
    h1_32 = ap(nc.dram_tensor("h1_32", [HF, 128, HF, 128], F32, kind="ExternalInput"))
    h2_32 = ap(nc.dram_tensor("h2_32", [HF, 128, HF, 128], F32, kind="ExternalInput"))
    w3_32 = ap(nc.dram_tensor("w3_32", [DF, 128, HF, 128], F32, kind="ExternalInput"))
    vecs_in = ap(nc.dram_tensor("vecs_in", [128, NV], F32, kind="ExternalInput"))
    meta_in = ap(nc.dram_tensor("meta_in", [1, 2], I32, kind="ExternalInput"))

    out_t = ap(nc.dram_tensor("out", [D, Q], F32, kind="ExternalOutput"))

    spill_kind = "ExternalOutput" if debug else "Internal"
    wb_names = ("wproj", "wdelta", "w1", "w2", "h1", "h2", "w3")
    wb_src = (wproj32, wdelta32, w1_32, w2_32, h1_32, h2_32, w3_32)
    wb_shapes = ([3 * DF, 128, DF, 128], [DF, 128, DF, 128], [HF, 128, DF, 128],
                 [HF, 128, DF, 128], [HF, 128, HF, 128], [HF, 128, HF, 128],
                 [DF, 128, HF, 128])
    wb = {nm: ap(nc.dram_tensor(nm + "_b", sh, BF16, kind="Internal"))
          for nm, sh in zip(wb_names, wb_shapes)}
    sp_dA = ap(nc.dram_tensor("sp_dA", [D, Q], F32, kind=spill_kind))
    sp_dB = ap(nc.dram_tensor("sp_dB", [D, Q], F32, kind=spill_kind))
    sp_res = ap(nc.dram_tensor("sp_res", [D, Q], F32, kind=spill_kind))

    xchg = ap(nc.dram_tensor("xchg", [2, 128, 48], F32, kind="Internal", addr_space="Shared"))
    flg = ap(nc.dram_tensor("flg", [2, 16], I32, kind="Internal", addr_space="Shared"))
    dbg = {}
    if debug:
        dbg["acc"] = ap(nc.dram_tensor("dbg_acc", [128, 48], F32, kind="ExternalOutput"))
        dbg["ucc"] = ap(nc.dram_tensor("dbg_ucc", [128, 32], F32, kind="ExternalOutput"))

    def dram_fm(t, c):
        return t[:, c * T:(c + 1) * T].rearrange("(f p) t -> p f t", p=128)

    def sb3(t):
        return t.ap().rearrange("p (f t) -> p f t", t=T)

    with ExitStack() as top:
        e = top.enter_context
        vecs = e(nc.sbuf_tensor("vecs", [128, NV], F32))
        acc = e(nc.sbuf_tensor("acc", [128, 48], F32))
        pacc = e(nc.sbuf_tensor("pacc", [128, 48], F32))
        ucc = e(nc.sbuf_tensor("ucc", [128, 32], F32))
        ones_cb = e(nc.sbuf_tensor("ones_cb", [128, 1], BF16))
        ones_rf = e(nc.sbuf_tensor("ones_rf", [1, 128], F32))
        meta_s = e(nc.sbuf_tensor("meta_s", [1, 2], I32))
        flg_s = e(nc.sbuf_tensor("flg_s", [1, 16], I32))
        flg_r = e(nc.sbuf_tensor("flg_r", [1, 16], I32))
        t1_row = [e(nc.sbuf_tensor(f"t1_row{s}", [1, T], F32)) for s in range(2)]
        var_row = [e(nc.sbuf_tensor(f"var_row{s}", [1, T], F32)) for s in range(2)]
        r_row = [e(nc.sbuf_tensor(f"r_row{s}", [1, T], F32)) for s in range(2)]
        mr_row = [e(nc.sbuf_tensor(f"mr_row{s}", [1, T], F32)) for s in range(2)]
        col_t = e(nc.sbuf_tensor("col_t", [128, 1], F32))
        dB_sl = [(e(nc.sbuf_tensor(f"db{i}", [128, T], F32)), Res()) for i in range(2)]
        ub = e(nc.sbuf_tensor("ub", [128, 128], F32))
        cb = e(nc.sbuf_tensor("cb", [128, 128], F32))
        sb_red = e(nc.sbuf_tensor("sb_red", [128, 64], F32))
        sb_own = e(nc.sbuf_tensor("sb_own", [128, 64], F32))
        ccs_buf = e(nc.sbuf_tensor("ccs_buf", [128, 64], F32))

        ps_g = [e(nc.psum_tensor(f"psg{i}", [128, T], F32)) for i in range(4)]
        ps_stat = [e(nc.psum_tensor(f"ps_stat{s}", [128, T], F32)) for s in range(2)]
        ps_mr = [e(nc.psum_tensor(f"ps_mr{s}", [128, T], F32)) for s in range(2)]

        sems = {}
        for nm in ("s_ld s_x g_cast s_wat t_mm t_stats t_bc a_ev a_row v_misc v_row v_fin "
                   "v_ap v_uc v_dl v_dA g_sp g_wr g_fl g_po g_rd v_cmb s2_ld w_ld "
                   "a2_ev v2_a g_out dbgst v_acc g_sq").split():
            sems[nm] = e(nc.semaphore(nm))

        r_bank = [Res() for _ in range(4)]
        rA = [Res(), Res()]
        rB = [Res(), Res()]
        rt1 = [Res(), Res()]
        rvr = [Res(), Res()]
        rrr = [Res(), Res()]
        rmr = [Res(), Res()]

        glb = {"li": 0}

        def ln_front(B, cast_sl, sq_sl, src_f, src_evs, vsig, asig,
                     cast_eng="vector"):
            s = glb["li"] % 2
            glb["li"] += 1
            stat_last = []
            last_cast = None
            for f in range(DF):
                cs, cr = cast_sl[f % 2]
                if cast_eng == "vector":
                    ev = B.op("vector",
                              (lambda f=f, cs=cs: lambda v: v.tensor_copy(cs[:, :], src_f(f)))(),
                              waits=cr.wr() + (list(src_evs) if f == 0 else []), sig=vsig)
                else:
                    ev = B.op("scalar",
                              (lambda f=f, cs=cs: lambda s_: s_.activation(cs[:, :], src_f(f), AF.Copy))(),
                              waits=cr.wr() + (list(src_evs) if f == 0 else []), sig=asig)
                last_cast = ev
                cr.note_w(ev)
                qs, qr = sq_sl[f % 2]
                ev2 = B.op("scalar",
                           (lambda cs=cs, qs=qs: lambda s_: s_.activation(qs[:, :], cs[:, :], AF.Square))(),
                           waits=[ev] + qr.wr(), sig=asig)
                cr.note_r(ev2)
                qr.note_w(ev2)
                mm1 = B.op("tensor",
                           (lambda f=f, cs=cs, s=s: lambda t: t.matmul(
                               ps_stat[s][0:1, :], ones_cb[:, 0:1], cs[:, :],
                               start=(f == 0), stop=(f == DF - 1)))(),
                           waits=[ev] + (rA[s].wr() + [glb.get("ones")] if f == 0 else []), sig="t_stats")
                cr.note_r(mm1)
                if f == 0:
                    rA[s].note_w(mm1)
                mm2 = B.op("tensor",
                           (lambda f=f, qs=qs, s=s: lambda t: t.matmul(
                               ps_stat[s][32:33, :], ones_cb[:, 0:1], qs[:, :],
                               start=(f == 0), stop=(f == DF - 1)))(),
                           waits=[ev2], sig="t_stats")
                qr.note_r(mm2)
                if f == DF - 1:
                    stat_last = [mm1, mm2]
            return dict(s=s, stat_last=stat_last, last_cast=last_cast,
                        B=B, vsig=vsig, asig=asig)

        def ln_rows(fr):
            B, s, vsig, asig = fr["B"], fr["s"], fr["vsig"], fr["asig"]
            stat_last = fr["stat_last"]
            ev = B.op("vector", (lambda s=s: lambda v: v.tensor_scalar(
                t1_row[s][:, :], ps_stat[s][0:1, :], INVD, None, ALU.mult))(),
                waits=stat_last + rt1[s].wr(), sig=vsig)
            rA[s].note_r(ev)
            rt1[s].note_w(ev)
            B.op("vector", (lambda s=s: lambda v: v.tensor_mul(
                mr_row[s][:, :], t1_row[s][:, :], t1_row[s][:, :]))(), waits=rmr[s].wr())
            evq = B.op("vector", (lambda s=s: lambda v: v.tensor_scalar(
                var_row[s][:, :], ps_stat[s][32:33, :], INVD, EPS, ALU.mult, ALU.add))(),
                waits=rvr[s].wr(), sig=vsig)
            rA[s].note_r(evq)
            ev_v = B.op("vector", (lambda s=s: lambda v: v.tensor_sub(
                var_row[s][:, :], var_row[s][:, :], mr_row[s][:, :]))(), sig="v_row")
            rvr[s].note_w(ev_v)
            ev_l = B.op("scalar", (lambda s=s: lambda s_: s_.activation(
                r_row[s][:, :], var_row[s][:, :], AF.Ln))(),
                waits=[ev_v] + rrr[s].wr(), sig=asig)
            rvr[s].note_r(ev_l)
            ev_r = B.op("scalar", (lambda s=s: lambda s_: s_.activation(
                r_row[s][:, :], r_row[s][:, :], AF.Exp, scale=-0.5))(), sig="a_row")
            rrr[s].note_w(ev_r)
            ev_m = B.op("vector", (lambda s=s: lambda v: v.tensor_mul(
                mr_row[s][:, :], t1_row[s][:, :], r_row[s][:, :]))(),
                waits=[ev_r], sig="v_fin")
            rrr[s].note_r(ev_m)
            rt1[s].note_r(ev_m)
            rmr[s].note_w(ev_m)
            fr["ev_m"] = ev_m
            return fr

        def ln_back(B, fr, apply_fn, apply_sig):
            s = fr["s"]
            bc1 = B.op("tensor", (lambda s=s: lambda t: t.matmul(
                ps_stat[s][:, :], ones_rf[0:1, :], r_row[s][0:1, :], start=True, stop=True))(),
                waits=[fr["ev_m"], glb.get("ones")] + rA[s].wr(), sig="t_bc")
            rrr[s].note_r(bc1)
            rA[s].note_w(bc1)
            bc2 = B.op("tensor", (lambda s=s: lambda t: t.matmul(
                ps_mr[s][:, :], ones_rf[0:1, :], mr_row[s][0:1, :], start=True, stop=True))(),
                waits=rB[s].wr(), sig="t_bc")
            rmr[s].note_r(bc2)
            rB[s].note_w(bc2)
            rb_ap, mb_ap = ps_stat[s], ps_mr[s]
            last = None
            for f in range(DF):
                last = apply_fn(f, [bc1, bc2] if f == 0 else [],
                                apply_sig if f == DF - 1 else None, rb_ap, mb_ap)
            rA[s].note_r(last)
            rB[s].note_r(last)
            return last

        def emit_ln(B, cast_sl, sq_sl, src_f, src_evs, apply_fn, apply_sig,
                    vsig, asig, cast_eng="vector"):
            fr = ln_front(B, cast_sl, sq_sl, src_f, src_evs, vsig, asig, cast_eng)
            ln_rows(fr)
            return ln_back(B, fr, apply_fn, apply_sig)

        B = Sched()
        with ExitStack() as p1:
            e1 = p1.enter_context
            xb = [e1(nc.sbuf_tensor(f"xb{i}", [128, DF * T], F32)) for i in range(2)]
            r_xb = [Res(), Res()]
            cast_sl = [(e1(nc.sbuf_tensor(f"cb{i}", [128, T], BF16)), Res()) for i in range(2)]
            sq_sl = [(e1(nc.sbuf_tensor(f"qb{i}", [128, T], BF16)), Res()) for i in range(2)]
            xtmp = [e1(nc.sbuf_tensor(f"xt{i}", [128, T], F32)) for i in range(2)]
            xnbf = e1(nc.sbuf_tensor("xnbf", [128, DF * T], BF16))
            dpre = e1(nc.sbuf_tensor("dpre", [128, DF * T], BF16))
            cs_sl = [(e1(nc.sbuf_tensor(f"csl{i}", [128, T], F32)), Res()) for i in range(2)]
            e_sl = [(e1(nc.sbuf_tensor(f"esl{i}", [128, T], F32)), Res()) for i in range(2)]
            eA_sl = [(e1(nc.sbuf_tensor(f"easl{i}", [128, T], F32)), Res()) for i in range(2)]
            s_sl = [e1(nc.sbuf_tensor(f"ssl{i}", [128, T], F32)) for i in range(2)]
            delta_sl = [(e1(nc.sbuf_tensor(f"dl{i}", [128, T], F32)), Res()) for i in range(2)]
            dA_sl = [(e1(nc.sbuf_tensor(f"da{i}", [128, T], F32)), Res()) for i in range(2)]
            res_sl = [(e1(nc.sbuf_tensor(f"re{i}", [128, T], F32)), Res()) for i in range(2)]
            junk = e1(nc.sbuf_tensor("junk", [128, T], F32))
            watn = e1(nc.sbuf_tensor("watn", [128, 4 * DF * DF * 128], BF16))

            r_xnbf, r_dpre = Res(), Res()

            def watile(m, k):
                o = (m * DF + k) * 128
                return watn[:, o:o + 128]

            ev_vec = B.op("sync", lambda s: s.dma_start(out=vecs[:, :], in_=vecs_in), sig="s_ld", dk=16)
            B.op("sync", lambda s: s.dma_start(out=meta_s[:, :], in_=meta_in), sig="s_ld", dk=16)
            B.op("vector", lambda v: v.memset(ones_cb[:, :], 1.0), sig="v_misc")
            B.op("vector", lambda v: v.memset(ones_rf[:, :], 1.0), sig="v_misc")
            B.op("vector", lambda v: v.memset(acc[:, :], 0.0), sig="v_misc")
            B.op("vector", lambda v: v.memset(ub[:, :], 0.0), sig="v_misc")
            B.op("vector", lambda v: v.memset(cb[:, :], 0.0), sig="v_misc")
            B.op("vector", lambda v: v.memset(sb_red[:, :], 0.0), sig="v_misc")
            ev_init = B.op("vector", lambda v: v.memset(sb_own[:, :], 0.0), sig="v_misc")
            glb["ones"] = ev_init

            cast_evs = {}

            def emit_cast_piece(nm, mlo, mhi):
                srcw = dict(zip(wb_names, wb_src))[nm]
                cast_evs[nm] = B.op("gpsimd",
                                    (lambda srcw=srcw, dst=wb[nm], mlo=mlo, mhi=mhi:
                                     lambda g: g.dma_start(out=dst[mlo:mhi], in_=srcw[mlo:mhi]))(),
                                    sig="g_cast", dk=16)

            emit_cast_piece("wproj", 0, 3 * DF)
            emit_cast_piece("wdelta", 0, DF)
            cast_pieces = ([("w1", 16 * i, 16 * i + 16) for i in range(2)]
                           + [("w2", 16 * i, 16 * i + 16) for i in range(2)]
                           + [("h1", 4 * i, 4 * i + 4) for i in range(8)]
                           + [("h2", 4 * i, 4 * i + 4) for i in range(8)]
                           + [("w3", 0, DF)])
            ev_wat1 = B.op("scalar", lambda s: s.dma_start(
                out=watn.ap()[:, 0:3 * DF * DF * 128].rearrange("p (m k j) -> p m k j", k=DF, j=128),
                in_=wb["wproj"].rearrange("m p k j -> p m k j")),
                waits=[cast_evs["wproj"]], sig="s_wat", dk=16)
            ev_wat2 = B.op("scalar", lambda s: s.dma_start(
                out=watn.ap()[:, 3 * DF * DF * 128:].rearrange("p (m k j) -> p m k j", k=DF, j=128),
                in_=wb["wdelta"].rearrange("m p k j -> p m k j")),
                waits=[cast_evs["wdelta"]], sig="s_wat", dk=16)

            sched = [("own", c) for c in range(NCH)] + [("red", c) for c in range(NCH)]
            for g, (kind, c) in enumerate(sched):
                for _ in range(2):
                    if cast_pieces:
                        emit_cast_piece(*cast_pieces.pop(0))
                src_x = x_own if kind == "own" else x_red
                xcur, rx = xb[g % 2], r_xb[g % 2]
                ev_x = B.op("sync", (lambda src_x=src_x, c=c, xcur=xcur: lambda s: s.dma_start(
                    out=sb3(xcur), in_=dram_fm(src_x, c)))(),
                    waits=rx.wr(), sig="s_x", dk=16)
                rx.note_w(ev_x)

                def apply_ln1(f, w0, sig, rb, mb, xcur=xcur, rx=rx):
                    B.op("vector", (lambda f=f, xcur=xcur, rb=rb: lambda v: v.tensor_mul(
                        xtmp[f % 2][:, :], xcur[:, f * T:(f + 1) * T], rb[:, :]))(),
                        waits=w0 + (r_xnbf.wr() if f == 0 else []))
                    ev2 = B.op("vector", (lambda f=f, mb=mb: lambda v: v.tensor_sub(
                        xnbf[:, f * T:(f + 1) * T], xtmp[f % 2][:, :], mb[:, :]))(), sig=sig)
                    return ev2

                ev_ap = emit_ln(B, cast_sl, sq_sl,
                                (lambda xcur=xcur: lambda f: xcur[:, f * T:(f + 1) * T])(),
                                [ev_x], apply_ln1, "v_ap", "v_misc", "a_ev",
                                cast_eng="scalar")
                rx.note_r(ev_ap)
                r_xnbf.note_w(ev_ap)

                n_m = 24 if kind == "own" else 16
                ev_dpre_done = None
                ev_res_done = None
                for m in range(n_m):
                    bank, rb = ps_g[m % 2], r_bank[m % 2]
                    mm = None
                    for k in range(DF):
                        mm = B.op("tensor", (lambda m=m, k=k, bank=bank: lambda t: t.matmul(
                            bank[:, :], watile(m, k), xnbf[:, k * T:(k + 1) * T],
                            start=(k == 0), stop=(k == DF - 1)))(),
                            waits=(([ev_ap, ev_wat1, ev_wat2] + rb.wr()) if k == 0 else []),
                            sig=("t_mm" if k == DF - 1 else None))
                    rb.note_w(mm)
                    r_xnbf.note_r(mm)
                    if m < 8:
                        evv = B.op("scalar", (lambda m=m, bank=bank: lambda s: s.activation(
                            dpre[:, m * T:(m + 1) * T], bank[:, :], AF.Copy))(),
                            waits=[mm] + (r_dpre.wr() if m == 0 else []), sig="a_ev")
                        r_dpre.note_w(evv)
                        rb.note_r(evv)
                        if m == 7:
                            ev_dpre_done = evv
                    elif m < 16:
                        cs, cr = cs_sl[m % 2]
                        evv = B.op("scalar", (lambda cs=cs, bank=bank: lambda s: s.activation(
                            cs[:, :], bank[:, :], AF.Copy))(),
                            waits=[mm] + cr.wr(), sig="a_ev")
                        cr.note_w(evv)
                        rb.note_r(evv)
                        base = 0 if kind == "own" else 64
                        f = m - 8
                        pc = base + c * 8 + f
                        B.op("vector", (lambda cs=cs, f=f, pc=pc: lambda v: v.scalar_tensor_tensor(
                            junk[:, :], cs[:, :], 1.0, xnbf[:, f * T:(f + 1) * T],
                            ALU.bypass, ALU.mult, accum_out=ub[:, pc:pc + 1]))(),
                            waits=[evv])
                        e_u2 = B.op("vector", (lambda cs=cs, pc=pc: lambda v: v.reduce_sum(
                            cb[:, pc:pc + 1], cs[:, :], axis=AX.X))(), sig="v_uc")
                        cr.note_r(e_u2)
                    else:
                        f = m - 16
                        rs, rr = res_sl[f % 2]
                        evv = B.op("scalar", (lambda rs=rs, bank=bank: lambda s: s.activation(
                            rs[:, :], bank[:, :], AF.Copy))(),
                            waits=[mm] + rr.wr(), sig="a_ev")
                        rr.note_w(evv)
                        rb.note_r(evv)
                        ev_sp = B.op("gpsimd", (lambda rs=rs, f=f, c=c: lambda g_: g_.dma_start(
                            out=sp_res[f * 128:(f + 1) * 128, c * T:(c + 1) * T], in_=rs[:, :]))(),
                            waits=[evv], sig="g_sp", dk=16)
                        rr.note_r(ev_sp)

                ev_sc = None
                ev_da = None
                for m in range(DF):
                    bank, rb = ps_g[m % 2], r_bank[m % 2]
                    mm = None
                    for k in range(DF):
                        mm = B.op("tensor", (lambda m=m, k=k, bank=bank: lambda t: t.matmul(
                            bank[:, :], watile(24 + m, k), dpre[:, k * T:(k + 1) * T],
                            start=(k == 0), stop=(k == DF - 1)))(),
                            waits=(([ev_dpre_done] + rb.wr()) if k == 0 else []),
                            sig=("t_mm" if k == DF - 1 else None))
                    rb.note_w(mm)
                    r_dpre.note_r(mm)
                    es_, er_ = e_sl[m % 2]
                    ev_u = B.op("scalar", (lambda es_=es_, bank=bank, m=m: lambda s: s.activation(
                        es_[:, :], bank[:, :], AF.Exp,
                        bias=vecs[:, VC_BDELTA + m:VC_BDELTA + m + 1]))(),
                        waits=[mm, ev_vec] + er_.wr(), sig="a_ev")
                    er_.note_w(ev_u)
                    rb.note_r(ev_u)
                    ds, dr = delta_sl[m % 2]
                    ev_dl = B.op("scalar", (lambda es_=es_, ds=ds, m=m: lambda s: s.activation(
                        ds[:, :], es_[:, :], AF.Ln, bias=1.0))(),
                        waits=dr.wr(), sig="v_dl")
                    er_.note_r(ev_dl)
                    dr.note_w(ev_dl)
                    ev_s = B.op("vector", (lambda ds=ds, m=m: lambda v: v.tensor_scalar_mul(
                        s_sl[m % 2][:, :], ds[:, :],
                        vecs[:, VC_B1 + m:VC_B1 + m + 1]))(), waits=[ev_vec, ev_dl], sig="v_misc")
                    dr.note_r(ev_s)
                    if kind == "own":
                        bs, br_ = dB_sl[m % 2]
                        ev_sc = B.op("vector", (lambda bs=bs, m=m: lambda v: v.tensor_tensor_scan(
                            bs[:, :], s_sl[m % 2][:, :], s_sl[m % 2][:, :],
                            0.0, ALU.add, ALU.bypass))(),
                            waits=br_.wr(), sig="v_misc")
                        br_.note_w(ev_sc)
                        pc = c * 8 + m
                        B.op("vector", (lambda m=m, pc=pc: lambda v: v.reduce_sum(
                            sb_own[:, pc:pc + 1], s_sl[m % 2][:, :], axis=AX.X))())
                        ev_spb = B.op("gpsimd", (lambda bs=bs, m=m, c=c: lambda g_: g_.dma_start(
                            out=sp_dB[m * 128:(m + 1) * 128, c * T:(c + 1) * T], in_=bs[:, :]))(),
                            waits=[ev_sc], sig="g_sp", dk=16)
                        br_.note_r(ev_spb)
                        es, er = eA_sl[m % 2]
                        ev_e = B.op("scalar", (lambda es=es, ds=ds, m=m: lambda s: s.activation(
                            es[:, :], ds[:, :], AF.Exp,
                            scale=vecs[:, VC_A1 + m:VC_A1 + m + 1]))(),
                            waits=[ev_dl] + er.wr(), sig="a_ev")
                        er.note_w(ev_e)
                        dr.note_r(ev_e)
                        as_, ar_ = dA_sl[m % 2]
                        ev_da = B.op("vector", (lambda es=es, as_=as_, m=m: lambda v: v.tensor_scalar_mul(
                            as_[:, :], es[:, :],
                            vecs[:, VC_B1 + m:VC_B1 + m + 1]))(),
                            waits=[ev_e] + ar_.wr(), sig="v_dA")
                        er.note_r(ev_da)
                        ar_.note_w(ev_da)
                        ev_spa = B.op("gpsimd", (lambda as_=as_, m=m, c=c: lambda g_: g_.dma_start(
                            out=sp_dA[m * 128:(m + 1) * 128, c * T:(c + 1) * T], in_=as_[:, :]))(),
                            waits=[ev_da], sig="g_sp", dk=16)
                        ar_.note_r(ev_spa)
                    else:
                        pc = c * 8 + m
                        B.op("vector", (lambda m=m, pc=pc: lambda v: v.reduce_sum(
                            sb_red[:, pc:pc + 1], s_sl[m % 2][:, :], axis=AX.X))())


            while cast_pieces:
                emit_cast_piece(*cast_pieces.pop(0))

            B.check()
            with nc.Block("A") as blkA:
                blkA.sync(lambda h: B.replay("sync", h, sems))
                blkA.tensor(lambda h: B.replay("tensor", h, sems))
                blkA.vector(lambda h: B.replay("vector", h, sems))
                blkA.scalar(lambda h: B.replay("scalar", h, sems))
                blkA.gpsimd(lambda h: B.replay("gpsimd", h, sems))

            def _bva(v):
                tgts = []
                for m in range(DF):
                    tgts.append((AC_U_OWN + m, ub, m))
                    tgts.append((AC_C_OWN + m, cb, m))
                    tgts.append((AC_U_RED + m, ub, 64 + m))
                    tgts.append((AC_C_RED + m, cb, 64 + m))
                    tgts.append((AC_S_RED + m, sb_red, m))
                    tgts.append((AC_S_OWN + m, sb_own, m))
                for col, srcb, base in tgts:
                    v.tensor_add(acc[:, col:col + 1], srcb[:, base:base + 1], srcb[:, base + 8:base + 9])
                for cc_ in range(2, 8):
                    for col, srcb, base in tgts:
                        v.tensor_add(acc[:, col:col + 1], acc[:, col:col + 1],
                                     srcb[:, base + 8 * cc_:base + 8 * cc_ + 1])
                for _ in range(3):
                    v.nop(cycle_cnt=250)
                v.engine_nop().then_inc(sems["v_acc"], 1)
                _bv(v)

            def _bg(g_):
                g_.wait_ge(sems["v_acc"], 1)
                with g_.register("pos") as pos, g_.register("nonce") as nonce:
                    g_.load(pos, meta_s[0:1, 0:1])
                    g_.load(nonce, meta_s[0:1, 1:2])
                    g_.memset(flg_s[:, :], 0)
                    g_.store(flg_s[0:1, 0:1], nonce)
                    for p in range(2):
                        with g_.If_eq(pos, p):
                            g_.dma_start(out=xchg[p, :, :], in_=acc[:, :]).then_inc(sems["g_wr"], 16)
                    g_.wait_ge(sems["g_wr"], 16)
                    for p in range(2):
                        with g_.If_eq(pos, p):
                            g_.dma_start(out=flg[p:p + 1, :], in_=flg_s[0:1, :]).then_inc(sems["g_fl"], 16)
                    g_.wait_ge(sems["g_fl"], 16)
                    with g_.register("cond") as cond, g_.register("cnt") as cnt, g_.register("fv") as fv:
                        g_.reg_mov(cond, 1)
                        g_.reg_mov(cnt, 16)
                        with g_.While(cond):
                            for p in range(2):
                                with g_.If_eq(pos, p):
                                    g_.dma_start(out=flg_r[:, :], in_=flg[1 - p:2 - p, :]).then_inc(sems["g_po"], 16)
                            g_.wait_ge(sems["g_po"], cnt)
                            g_.reg_add(cnt, cnt, 16)
                            g_.load(fv, flg_r[0:1, 0:1])
                            g_.reg_alu(cond, fv, nonce, ALU.not_equal)
                    for p in range(2):
                        with g_.If_eq(pos, p):
                            g_.dma_start(out=pacc[:, :], in_=xchg[1 - p, :, :]).then_inc(sems["g_rd"], 16)
                    g_.wait_ge(sems["g_rd"], 16)

            def _bv(v):
                v.wait_ge(sems["g_rd"], 16)

                def space():
                    for _ in range(3):
                        v.nop(cycle_cnt=250)
                v.tensor_add(ucc[:, 0:8], acc[:, AC_U_OWN:AC_U_OWN + 8], acc[:, AC_U_RED:AC_U_RED + 8])
                v.tensor_add(ucc[:, 8:16], acc[:, AC_C_OWN:AC_C_OWN + 8], acc[:, AC_C_RED:AC_C_RED + 8])
                v.tensor_scalar_mul(ucc[:, 24:32], acc[:, AC_S_RED:AC_S_RED + 8], vecs[:, VC_CA:VC_CA + 1])
                space()
                v.tensor_add(ucc[:, 0:8], ucc[:, 0:8], pacc[:, AC_U_OWN:AC_U_OWN + 8])
                v.tensor_add(ucc[:, 8:16], ucc[:, 8:16], pacc[:, AC_C_OWN:AC_C_OWN + 8])
                v.scalar_tensor_tensor(ucc[:, 24:32], pacc[:, AC_S_OWN:AC_S_OWN + 8],
                                       vecs[:, VC_CB:VC_CB + 1], ucc[:, 24:32], ALU.mult, ALU.add)
                space()
                v.tensor_add(ucc[:, 0:8], ucc[:, 0:8], pacc[:, AC_U_RED:AC_U_RED + 8])
                v.tensor_add(ucc[:, 8:16], ucc[:, 8:16], pacc[:, AC_C_RED:AC_C_RED + 8])
                v.scalar_tensor_tensor(ucc[:, 24:32], pacc[:, AC_S_RED:AC_S_RED + 8],
                                       vecs[:, VC_CC:VC_CC + 1], ucc[:, 24:32], ALU.mult, ALU.add)
                space()
                for c_ in range(NCH):
                    v.tensor_mul(ccs_buf[:, c_ * 8:(c_ + 1) * 8], ucc[:, 24:32], ucc[:, 8:16])
                    v.tensor_add(ucc[:, 24:32], ucc[:, 24:32], sb_own[:, c_ * 8:(c_ + 1) * 8])
                    space()
                v.engine_nop().then_inc(sems["v_cmb"], 1)


        B2 = Sched(counts=B.counts)
        with ExitStack() as p3:
            e3 = p3.enter_context
            R1 = e3(nc.sbuf_tensor("R1", [128, DF * T], F32))
            R2 = [e3(nc.sbuf_tensor(f"R2_{i}", [128, DF * T], F32)) for i in range(2)]
            R3 = [e3(nc.sbuf_tensor(f"R3_{i}", [128, DF * T], F32)) for i in range(2)]
            R4 = [e3(nc.sbuf_tensor(f"R4_{i}", [128, DF * T], F32)) for i in range(2)]
            R5 = e3(nc.sbuf_tensor("R5", [128, DF * T], F32))
            R6 = e3(nc.sbuf_tensor("R6", [128, DF * T], BF16))
            cast2 = [(e3(nc.sbuf_tensor(f"c2_{i}", [128, T], BF16)), Res()) for i in range(2)]
            sq2 = [(e3(nc.sbuf_tensor(f"q2_{i}", [128, T], BF16)), Res()) for i in range(2)]
            xtmp2 = [e3(nc.sbuf_tensor(f"xt2_{i}", [128, T], F32)) for i in range(2)]
            gsl = [(e3(nc.sbuf_tensor(f"gg{i}", [128, T], F32)), Res()) for i in range(2)]
            wbuf = e3(nc.sbuf_tensor("wbuf", [128, 4 * HF * 128], BF16))
            r_wslot = [Res() for _ in range(4)]
            r_R1, r_R5, r_R6 = Res(), Res(), Res()
            r_R2 = [Res(), Res()]
            r_R3 = [Res(), Res()]
            r_R4 = [Res(), Res()]

            def wtile(s, k):
                o = s * HF * 128 + k * 128
                return wbuf[:, o:o + 128]

            R1b = R1.ap().bitcast(BF16)
            R2b = [R2[i].ap().bitcast(BF16) for i in range(2)]
            R3b = [R3[i].ap().bitcast(BF16) for i in range(2)]
            R4b = [R4[i].ap().bitcast(BF16) for i in range(2)]

            def a_tile(m, p):
                bb = R2b[p] if m < 16 else R3b[p]
                return bb[:, (m % 16) * T:((m % 16) + 1) * T]

            def b_tile(m, p):
                bb = R4b[p] if m < 16 else R1b
                return bb[:, (m % 16) * T:((m % 16) + 1) * T]

            c_tile = a_tile
            slot_i = 0

            def _bva_wrap(v):
                _bva(v)
                return None
            B2.counts["v_cmb"] = B2.counts.get("v_cmb", 0) + 1
            ev_cmb = Ev("v_cmb", B2.counts["v_cmb"], "vector", 1)
            B2.q["vector"].append(([], _bva_wrap, None, 1))
            B2.watermark["vector"]["v_cmb"] = B2.counts["v_cmb"]
            B2.q["gpsimd"].append(([], lambda g_: _bg(g_), None, 1))
            if debug:
                def _dbg_store(sy):
                    sy.wait_ge(sems["v_cmb"], 1)
                    sy.dma_start(out=dbg["acc"], in_=acc[:, :]).then_inc(sems["dbgst"], 16)
                    sy.dma_start(out=dbg["ucc"], in_=ucc[:, :]).then_inc(sems["dbgst"], 16)
                    sy.wait_ge(sems["dbgst"], 32)
                B2.q["sync"].append(([], _dbg_store, None, 1))

            def stream2(arr, kf, mlo, mhi, cast_ev=None):
                nonlocal slot_i
                s = slot_i % 4
                slot_i += 1
                ncols = (mhi - mlo) * kf * 128
                ev = B2.op("sync", (lambda arr=arr, mlo=mlo, mhi=mhi, s=s, ncols=ncols, kf=kf: lambda sy: sy.dma_start(
                    out=wbuf.ap()[:, s * HF * 128: s * HF * 128 + ncols].rearrange(
                        "p (m k j) -> p m k j", k=kf, j=128),
                    in_=arr[mlo:mhi].rearrange("m p k j -> p m k j")))(),
                    waits=r_wslot[s].wr() + [cast_ev], sig="w_ld", dk=16)
                r_wslot[s].note_w(ev)
                return ev, s

            def prologue(c):
                p = c % 2
                lv = {}
                for nm, R, rr, src in (("R2", R2[p], r_R2[p], sp_dA),
                                       ("R3", R3[p], r_R3[p], sp_dB),
                                       ("R4", R4[p], r_R4[p], sp_res)):
                    lv[nm] = B2.op("sync", (lambda R=R, src=src, c=c: lambda s: s.dma_start(
                        out=sb3(R), in_=dram_fm(src, c)))(),
                        waits=rr.wr(), sig="s2_ld", dk=16)
                    rr.note_w(lv[nm])
                ev_y = None
                for f in range(DF):
                    B2.op("vector", (lambda f=f, c=c, p=p: lambda v: v.tensor_scalar(
                        R4[p][:, f * T:(f + 1) * T], R4[p][:, f * T:(f + 1) * T],
                        vecs[:, VC_D:VC_D + 1], ccs_buf[:, c * 8 + f:c * 8 + f + 1],
                        ALU.mult, ALU.add))(),
                        waits=([lv["R4"], ev_cmb] if f == 0 else []))
                    B2.op("vector", (lambda f=f, p=p: lambda v: v.scalar_tensor_tensor(
                        R4[p][:, f * T:(f + 1) * T], R3[p][:, f * T:(f + 1) * T],
                        ucc[:, 8 + f:8 + f + 1], R4[p][:, f * T:(f + 1) * T], ALU.mult, ALU.add))(),
                        waits=[lv["R3"]] if f == 0 else [])
                    ev_y = B2.op("vector", (lambda f=f, p=p: lambda v: v.scalar_tensor_tensor(
                        R4[p][:, f * T:(f + 1) * T], R2[p][:, f * T:(f + 1) * T],
                        ucc[:, 0 + f:0 + f + 1], R4[p][:, f * T:(f + 1) * T], ALU.mult, ALU.add))(),
                        waits=[lv["R2"]] if f == 0 else [],
                        sig=("v2_a" if f == DF - 1 else None))
                r_R2[p].note_r(ev_y)
                r_R3[p].note_r(ev_y)
                r_R4[p].note_w(ev_y)
                fr_attn = ln_front(B2, cast2, sq2,
                                   (lambda p=p: lambda f: R4[p][:, f * T:(f + 1) * T])(),
                                   [ev_y], "v2_a", "a2_ev")
                return ev_y, fr_attn

            pro_y = prologue(0)
            for c in range(NCH):
                p = c % 2
                ev_y, fr_attn = pro_y
                lev_x = B2.op("sync", (lambda c=c: lambda s: s.dma_start(
                    out=sb3(R1), in_=dram_fm(x_own, c)))(),
                    waits=r_R1.wr(), sig="s2_ld", dk=16)
                r_R1.note_w(lev_x)

                def apply_attn(f, w0, sig, rb, mb):
                    B2.op("vector", (lambda f=f, p=p, rb=rb: lambda v: v.tensor_mul(
                        R5[:, f * T:(f + 1) * T], R4[p][:, f * T:(f + 1) * T], rb[:, :]))(),
                        waits=w0 + (r_R5.wr() if f == 0 else []))
                    B2.op("vector", (lambda f=f, mb=mb: lambda v: v.tensor_sub(
                        R5[:, f * T:(f + 1) * T], R5[:, f * T:(f + 1) * T], mb[:, :]))())
                    return B2.op("vector", (lambda f=f: lambda v: v.tensor_add(
                        R5[:, f * T:(f + 1) * T], R5[:, f * T:(f + 1) * T], R1[:, f * T:(f + 1) * T]))(),
                        waits=([lev_x] if f == 0 else []), sig=sig)

                ln_rows(fr_attn)
                ev_x1 = ln_back(B2, fr_attn, apply_attn, "v2_a")
                r_R4[p].note_r(ev_x1)
                r_R1.note_r(ev_x1)
                r_R5.note_w(ev_x1)

                def apply_ln2(f, w0, sig, rb, mb):
                    B2.op("vector", (lambda f=f, rb=rb: lambda v: v.tensor_mul(
                        xtmp2[f % 2][:, :], R5[:, f * T:(f + 1) * T], rb[:, :]))(),
                        waits=w0 + (r_R6.wr() if f == 0 else []))
                    return B2.op("vector", (lambda f=f, mb=mb: lambda v: v.tensor_sub(
                        R6[:, f * T:(f + 1) * T], xtmp2[f % 2][:, :], mb[:, :]))(), sig=sig)

                ev_xn1 = emit_ln(B2, cast2, sq2, lambda f: R5[:, f * T:(f + 1) * T],
                                 [ev_x1], apply_ln2, "v2_a", "v2_a", "a2_ev")
                r_R5.note_r(ev_xn1)
                r_R6.note_w(ev_xn1)

                ev_a_last = None
                for j in range(HF // 4):
                    e_w1, s1 = stream2(wb["w1"], DF, 4 * j, 4 * j + 4, cast_evs["w1"])
                    e_w2, s2 = stream2(wb["w2"], DF, 4 * j, 4 * j + 4, cast_evs["w2"])
                    for mi in range(4):
                        m = 4 * j + mi
                        b1, rb1 = ps_g[m % 2], r_bank[m % 2]
                        b2, rb2 = ps_g[2 + m % 2], r_bank[2 + m % 2]
                        mm1 = mm2 = None
                        for k in range(DF):
                            mm1 = B2.op("tensor", (lambda s1=s1, mi=mi, k=k, b1=b1: lambda t: t.matmul(
                                b1[:, :], wtile(s1, mi * DF + k), R6[:, k * T:(k + 1) * T],
                                start=(k == 0), stop=(k == DF - 1)))(),
                                waits=(([e_w1, ev_xn1] + rb1.wr()) if k == 0 else []),
                                sig=("t_mm" if k == DF - 1 else None))
                        rb1.note_w(mm1)
                        r_wslot[s1].note_r(mm1)
                        r_R6.note_r(mm1)
                        for k in range(DF):
                            mm2 = B2.op("tensor", (lambda s2=s2, mi=mi, k=k, b2=b2: lambda t: t.matmul(
                                b2[:, :], wtile(s2, mi * DF + k), R6[:, k * T:(k + 1) * T],
                                start=(k == 0), stop=(k == DF - 1)))(),
                                waits=(([e_w2] + rb2.wr()) if k == 0 else []),
                                sig=("t_mm" if k == DF - 1 else None))
                        rb2.note_w(mm2)
                        r_wslot[s2].note_r(mm2)
                        r_R6.note_r(mm2)
                        gs, gr = gsl[m % 2]
                        ev_g = B2.op("scalar", (lambda gs=gs, b1=b1, m=m: lambda s: s.activation(
                            gs[:, :], b1[:, :], AF.Silu,
                            scale=vecs[:, VC_W1S + m:VC_W1S + m + 1]))(),
                            waits=[mm1] + gr.wr(), sig="a2_ev")
                        gr.note_w(ev_g)
                        rb1.note_r(ev_g)
                        wx = [mm2, ev_g]
                        if m == 0:
                            wx += r_R2[p].wr()
                        if m == 16:
                            wx += r_R3[p].wr()
                        ev_a = B2.op("vector", (lambda gs=gs, b2=b2, m=m, p=p: lambda v: v.scalar_tensor_tensor(
                            a_tile(m, p), b2[:, :], vecs[:, VC_W2S + m:VC_W2S + m + 1],
                            gs[:, :], ALU.mult, ALU.mult))(),
                            waits=wx, sig="v2_a")
                        gr.note_r(ev_a)
                        rb2.note_r(ev_a)
                        if m == 0:
                            r_R2[p].note_w(ev_a)
                        if m == 16:
                            r_R3[p].note_w(ev_a)
                        ev_a_last = ev_a

                ev_b_last = None
                for m in range(HF):
                    e_w, s = stream2(wb["h1"], HF, m, m + 1, cast_evs["h1"])
                    bank, rb = ps_g[m % 4], r_bank[m % 4]
                    mm = None
                    for k in range(HF):
                        mm = B2.op("tensor", (lambda s=s, k=k, bank=bank, p=p: lambda t: t.matmul(
                            bank[:, :], wtile(s, k), a_tile(k, p),
                            start=(k == 0), stop=(k == HF - 1)))(),
                            waits=(([e_w, ev_a_last] + rb.wr()) if k == 0 else []),
                            sig=("t_mm" if k == HF - 1 else None))
                    rb.note_w(mm)
                    r_wslot[s].note_r(mm)
                    r_R2[p].note_r(mm)
                    r_R3[p].note_r(mm)
                    wx = [mm]
                    if m == 0:
                        wx += r_R4[p].wr()
                    if m == 16:
                        wx += r_R1.wr()
                    ev_b = B2.op("scalar", (lambda m=m, bank=bank, p=p: lambda s: s.activation(
                        b_tile(m, p), bank[:, :], AF.Gelu,
                        scale=vecs[:, VC_H1S + m:VC_H1S + m + 1]))(),
                        waits=wx, sig="a2_ev")
                    rb.note_r(ev_b)
                    if m == 0:
                        r_R4[p].note_w(ev_b)
                    if m == 16:
                        r_R1.note_w(ev_b)
                    ev_b_last = ev_b

                if c + 1 < NCH:
                    pro_y = prologue(c + 1)

                ev_c_last = None
                for m in range(HF):
                    e_w, s = stream2(wb["h2"], HF, m, m + 1, cast_evs["h2"])
                    bank, rb = ps_g[m % 4], r_bank[m % 4]
                    mm = None
                    for k in range(HF):
                        mm = B2.op("tensor", (lambda s=s, k=k, bank=bank, p=p: lambda t: t.matmul(
                            bank[:, :], wtile(s, k), b_tile(k, p),
                            start=(k == 0), stop=(k == HF - 1)))(),
                            waits=(([e_w, ev_b_last] + rb.wr()) if k == 0 else []),
                            sig=("t_mm" if k == HF - 1 else None))
                    rb.note_w(mm)
                    r_wslot[s].note_r(mm)
                    r_R4[p].note_r(mm)
                    r_R1.note_r(mm)
                    wx = [mm]
                    if m == 0:
                        wx += r_R2[p].wr()
                    if m == 16:
                        wx += r_R3[p].wr()
                    ev_c = B2.op("scalar", (lambda m=m, bank=bank, p=p: lambda s: s.activation(
                        c_tile(m, p), bank[:, :], AF.Gelu,
                        scale=vecs[:, VC_H2S + m:VC_H2S + m + 1]))(),
                        waits=wx, sig="a2_ev")
                    rb.note_r(ev_c)
                    if m == 0:
                        r_R2[p].note_w(ev_c)
                    if m == 16:
                        r_R3[p].note_w(ev_c)
                    ev_c_last = ev_c

                ev_f = None
                for m in range(DF):
                    e_w, s = stream2(wb["w3"], HF, m, m + 1, cast_evs["w3"])
                    bank, rb = ps_g[m % 4], r_bank[m % 4]
                    mm = None
                    for k in range(HF):
                        mm = B2.op("tensor", (lambda s=s, k=k, bank=bank, p=p: lambda t: t.matmul(
                            bank[:, :], wtile(s, k), c_tile(k, p),
                            start=(k == 0), stop=(k == HF - 1)))(),
                            waits=(([e_w, ev_c_last] + rb.wr()) if k == 0 else []),
                            sig=("t_mm" if k == HF - 1 else None))
                    rb.note_w(mm)
                    r_wslot[s].note_r(mm)
                    r_R2[p].note_r(mm)
                    r_R3[p].note_r(mm)
                    wx = [mm]
                    if m == 0:
                        wx += r_R4[p].wr()
                    ev_f = B2.op("scalar", (lambda m=m, bank=bank, p=p: lambda s: s.activation(
                        R4[p][:, m * T:(m + 1) * T], bank[:, :], AF.Copy,
                        scale=vecs[:, VC_W3S + m:VC_W3S + m + 1]))(),
                        waits=wx, sig="a2_ev")
                    rb.note_r(ev_f)
                    r_R4[p].note_w(ev_f)

                def apply_fl(f, w0, sig, rb, mb):
                    B2.op("vector", (lambda f=f, p=p, rb=rb: lambda v: v.tensor_mul(
                        R4[p][:, f * T:(f + 1) * T], R4[p][:, f * T:(f + 1) * T], rb[:, :]))(),
                        waits=w0)
                    B2.op("vector", (lambda f=f, p=p, mb=mb: lambda v: v.tensor_sub(
                        R4[p][:, f * T:(f + 1) * T], R4[p][:, f * T:(f + 1) * T], mb[:, :]))())
                    return B2.op("vector", (lambda f=f, p=p: lambda v: v.tensor_add(
                        R4[p][:, f * T:(f + 1) * T], R4[p][:, f * T:(f + 1) * T],
                        R5[:, f * T:(f + 1) * T]))(),
                        sig=sig)

                ev_out = emit_ln(B2, cast2, sq2,
                                 (lambda p=p: lambda f: R4[p][:, f * T:(f + 1) * T])(),
                                 [ev_f], apply_fl, "v2_a", "v2_a", "a2_ev")
                r_R4[p].note_w(ev_out)
                r_R5.note_r(ev_out)

                ev_st = B2.op("sync", (lambda c=c, p=p: lambda sy: sy.dma_start(
                    out=dram_fm(out_t, c), in_=sb3(R4[p])))(),
                    waits=[ev_out], sig="g_out", dk=16)
                r_R4[p].note_r(ev_st)

            gout_total = B2.counts.get("g_out", 0)
            B2.op("sync", (lambda gt=gout_total: lambda sy: sy.wait_ge(sems["g_out"], gt))())

            base_counts = dict(B.counts)
            base_counts["v_cmb"] = base_counts.get("v_cmb", 0) + 1
            B2.check(base=base_counts)
            with nc.Block("C") as blkC:
                blkC.sync(lambda h: B2.replay("sync", h, sems))
                blkC.tensor(lambda h: B2.replay("tensor", h, sems))
                blkC.vector(lambda h: B2.replay("vector", h, sems))
                blkC.scalar(lambda h: B2.replay("scalar", h, sems))
                blkC.gpsimd(lambda h: B2.replay("gpsimd", h, sems))

    return nc


def kernel(**inputs):
    global LAST
    x = np.asarray(inputs["L0_attn_x"])
    Wp = np.asarray(inputs["L0_attn_W_proj"])
    A = np.asarray(inputs["L0_attn_A"]).reshape(-1)
    Bp = np.asarray(inputs["L0_attn_Bp"]).reshape(-1)
    Dsc = np.asarray(inputs["L0_attn_D"]).reshape(-1)
    Wd = np.asarray(inputs["L0_attn_W_delta"])
    bd = np.asarray(inputs["L0_attn_b_delta"]).reshape(-1)
    w1w = np.asarray(inputs["L0_ffn_w1_w"]); w1s = np.asarray(inputs["L0_ffn_w1_s"])
    w2w = np.asarray(inputs["L0_ffn_w2_w"]); w2s = np.asarray(inputs["L0_ffn_w2_s"])
    h1w = np.asarray(inputs["L0_ffn_h1_w"]); h1s = np.asarray(inputs["L0_ffn_h1_s"])
    h2w = np.asarray(inputs["L0_ffn_h2_w"]); h2s = np.asarray(inputs["L0_ffn_h2_s"])
    w3w = np.asarray(inputs["L0_ffn_w3_w"]); w3s = np.asarray(inputs["L0_ffn_w3_s"])

    Bsz, L, _ = x.shape
    Q = L // 4
    nc = build_program(Q, debug=_DEBUG)

    wproj_used = np.concatenate([Wp[0:D], Wp[2 * D:3 * D], Wp[3 * D:4 * D]], axis=0)
    tw = {
        "wproj32": _tile_weight(wproj_used), "wdelta32": _tile_weight(Wd),
        "w1_32": _tile_weight(w1w), "w2_32": _tile_weight(w2w),
        "h1_32": _tile_weight(h1w), "h2_32": _tile_weight(h2w),
        "w3_32": _tile_weight(w3w),
    }
    vecs = np.zeros((128, NV), dtype=np.float32)
    vecs[:, VC_BDELTA:VC_BDELTA + DF] = _cols(bd)
    vecs[:, VC_A1:VC_A1 + DF] = _cols(A)
    vecs[:, VC_B1:VC_B1 + DF] = _cols(Bp)
    vecs[:, VC_W1S:VC_W1S + HF] = _cols(w1s)
    vecs[:, VC_W2S:VC_W2S + HF] = _cols(w2s)
    vecs[:, VC_H1S:VC_H1S + HF] = _cols(h1s)
    vecs[:, VC_H2S:VC_H2S + HF] = _cols(h2s)
    vecs[:, VC_W3S:VC_W3S + DF] = _cols(w3s)
    vecs[:, VC_D] = Dsc[0]

    CARRY_COEF = {0: (0., 0., 0.), 1: (0., 1., 0.), 2: (1., 0., 1.), 3: (1., 1., 1.)}
    RED_Q = {0: 2, 1: 3, 2: 0, 3: 1}

    rng = np.random.default_rng()
    nonces = [int(rng.integers(1, 2**31 - 1)) for _ in range(4)]
    in_maps = []
    for core in range(8):
        b, r = core // 4, core % 4
        xt = np.ascontiguousarray(x[b].T)
        vc = vecs.copy()
        a_, b_, c_ = CARRY_COEF[r]
        vc[:, VC_CA], vc[:, VC_CB], vc[:, VC_CC] = a_, b_, c_
        in_maps.append({
            "x_own": np.ascontiguousarray(xt[:, r * Q:(r + 1) * Q]),
            "x_red": np.ascontiguousarray(xt[:, RED_Q[r] * Q:(RED_Q[r] + 1) * Q]),
            "vecs_in": vc,
            "meta_in": np.array([[core % 2, nonces[core // 2]]], dtype=np.int32),
            **tw,
        })

    res = run_bass_kernel_spmd(nc, in_maps, core_ids=list(range(8)))
    LAST = res

    out = np.empty_like(x)
    for core in range(8):
        b, r = core // 4, core % 4
        out[b, r * Q:(r + 1) * Q, :] = res.results[core]["out"].T
    return out


# version 21
# speedup vs baseline: 3.3075x; 1.2914x over previous
import numpy as np
from contextlib import ExitStack

import concourse.bass as bass
import concourse.mybir as mybir
from concourse.bass_utils import run_bass_kernel_spmd

F32 = mybir.dt.float32
BF16 = mybir.dt.bfloat16
I32 = mybir.dt.int32
AF = mybir.ActivationFunctionType
ALU = mybir.AluOpType
AX = mybir.AxisListType

D = 1024
DF = D // 128
H = 4096
HF = H // 128
T = 512
EPS = 1e-5
INVD = 1.0 / D

VC_BDELTA, VC_A1, VC_B1 = 0, 8, 16
VC_W1S, VC_W2S, VC_H1S, VC_H2S, VC_W3S = 24, 56, 88, 120, 152
VC_D, VC_CA, VC_CB, VC_CC = 160, 161, 162, 163
NV = 164
AC_U_OWN, AC_C_OWN, AC_S_OWN, AC_U_RED, AC_C_RED, AC_S_RED = 0, 8, 16, 24, 32, 40

ENGS = ("sync", "tensor", "vector", "scalar", "gpsimd")

_DEBUG = False
LAST = None


class Ev:
    __slots__ = ("sem", "val", "eng", "dk")

    def __init__(self, sem, val, eng, dk):
        self.sem, self.val, self.eng, self.dk = sem, val, eng, dk


class Sched:
    def __init__(self, counts=None):
        self.q = {e: [] for e in ENGS}
        self.counts = dict(counts or {})
        self.watermark = {e: {} for e in ENGS}

    def op(self, eng, fn, waits=(), sig=None, dk=1):
        ws = []
        wm = self.watermark[eng]
        for ev in waits:
            if ev is None:
                continue
            if ev.eng == eng and ev.dk == 1:
                continue
            if wm.get(ev.sem, 0) >= ev.val:
                continue
            wm[ev.sem] = ev.val
            ws.append((ev.sem, ev.val))
        out = None
        if sig is not None:
            self.counts[sig] = self.counts.get(sig, 0) + dk
            out = Ev(sig, self.counts[sig], eng, dk)
            if dk == 1:
                wm[sig] = self.counts[sig]
        self.q[eng].append((ws, fn, sig, dk))
        return out

    def check(self, base=None):
        pc = {e: 0 for e in ENGS}
        cnt = dict(base or {})
        progress = True
        while progress:
            progress = False
            for e in ENGS:
                while pc[e] < len(self.q[e]):
                    ws, _, sig, dk = self.q[e][pc[e]]
                    if any(cnt.get(s, 0) < v for s, v in ws):
                        break
                    if sig:
                        cnt[sig] = cnt.get(sig, 0) + dk
                    pc[e] += 1
                    progress = True
        stuck = {e: (len(self.q[e]) - pc[e]) for e in ENGS if pc[e] < len(self.q[e])}
        if stuck:
            det = {e: [(s, v, cnt.get(s, 0)) for s, v in self.q[e][pc[e]][0]] for e in stuck}
            raise RuntimeError(f"schedule deadlock remaining={stuck} blocked_on={det}")
        return cnt

    def replay(self, eng, handle, sems):
        for ws, fn, sig, dk in self.q[eng]:
            for s, v in ws:
                handle.wait_ge(sems[s], v)
            instr = fn(handle)
            if sig is not None and instr is not None:
                instr.then_inc(sems[sig], dk)


class Res:

    def __init__(self):
        self.w = None
        self.r = []

    def wr(self):
        return list(self.r) + ([self.w] if self.w else [])

    def note_w(self, ev):
        self.w = ev
        self.r = []

    def note_r(self, ev):
        if ev is not None:
            self.r.append(ev)


def _tile_weight(w):
    O, I = w.shape
    return np.ascontiguousarray(w.reshape(O // 128, 128, I // 128, 128).transpose(0, 3, 2, 1))


def _cols(vec):
    n = vec.shape[0] // 128
    return np.ascontiguousarray(vec.reshape(n, 128).T)


def build_program(Q, debug=False):
    NCH = Q // T
    nc = bass.Bass("TRN2", target_bir_lowering=False, detect_race_conditions=False)

    ap = lambda t: t.ap()
    x_own = ap(nc.dram_tensor("x_own", [D, Q], F32, kind="ExternalInput"))
    x_red = ap(nc.dram_tensor("x_red", [D, Q], F32, kind="ExternalInput"))
    wproj32 = ap(nc.dram_tensor("wproj32", [3 * DF, 128, DF, 128], F32, kind="ExternalInput"))
    wdelta32 = ap(nc.dram_tensor("wdelta32", [DF, 128, DF, 128], F32, kind="ExternalInput"))
    w1_32 = ap(nc.dram_tensor("w1_32", [HF, 128, DF, 128], F32, kind="ExternalInput"))
    w2_32 = ap(nc.dram_tensor("w2_32", [HF, 128, DF, 128], F32, kind="ExternalInput"))
    h1_32 = ap(nc.dram_tensor("h1_32", [HF, 128, HF, 128], F32, kind="ExternalInput"))
    h2_32 = ap(nc.dram_tensor("h2_32", [HF, 128, HF, 128], F32, kind="ExternalInput"))
    w3_32 = ap(nc.dram_tensor("w3_32", [DF, 128, HF, 128], F32, kind="ExternalInput"))
    vecs_in = ap(nc.dram_tensor("vecs_in", [128, NV], F32, kind="ExternalInput"))
    meta_in = ap(nc.dram_tensor("meta_in", [1, 2], I32, kind="ExternalInput"))

    out_t = ap(nc.dram_tensor("out", [D, Q], F32, kind="ExternalOutput"))

    spill_kind = "ExternalOutput" if debug else "Internal"
    wb_names = ("wproj", "wdelta", "w1", "w2", "h1", "h2", "w3")
    wb_src = (wproj32, wdelta32, w1_32, w2_32, h1_32, h2_32, w3_32)
    wb_shapes = ([3 * DF, 128, DF, 128], [DF, 128, DF, 128], [HF, 128, DF, 128],
                 [HF, 128, DF, 128], [HF, 128, HF, 128], [HF, 128, HF, 128],
                 [DF, 128, HF, 128])
    wb = {nm: ap(nc.dram_tensor(nm + "_b", sh, BF16, kind="Internal"))
          for nm, sh in zip(wb_names, wb_shapes)}
    sp_dA = ap(nc.dram_tensor("sp_dA", [D, Q], F32, kind=spill_kind))
    sp_dB = ap(nc.dram_tensor("sp_dB", [D, Q], F32, kind=spill_kind))
    sp_res = ap(nc.dram_tensor("sp_res", [D, Q], F32, kind=spill_kind))

    xchg = ap(nc.dram_tensor("xchg", [2, 128, 48], F32, kind="Internal", addr_space="Shared"))
    flg = ap(nc.dram_tensor("flg", [2, 16], I32, kind="Internal", addr_space="Shared"))
    dbg = {}
    if debug:
        dbg["acc"] = ap(nc.dram_tensor("dbg_acc", [128, 48], F32, kind="ExternalOutput"))
        dbg["ucc"] = ap(nc.dram_tensor("dbg_ucc", [128, 32], F32, kind="ExternalOutput"))

    def dram_fm(t, c):
        return t[:, c * T:(c + 1) * T].rearrange("(f p) t -> p f t", p=128)

    def sb3(t):
        return t.ap().rearrange("p (f t) -> p f t", t=T)

    with ExitStack() as top:
        e = top.enter_context
        vecs = e(nc.sbuf_tensor("vecs", [128, NV], F32))
        acc = e(nc.sbuf_tensor("acc", [128, 48], F32))
        pacc = e(nc.sbuf_tensor("pacc", [128, 48], F32))
        ucc = e(nc.sbuf_tensor("ucc", [128, 32], F32))
        ones_cb = e(nc.sbuf_tensor("ones_cb", [128, 1], BF16))
        ones_rf = e(nc.sbuf_tensor("ones_rf", [1, 128], F32))
        meta_s = e(nc.sbuf_tensor("meta_s", [1, 2], I32))
        flg_s = e(nc.sbuf_tensor("flg_s", [1, 16], I32))
        flg_r = e(nc.sbuf_tensor("flg_r", [1, 16], I32))
        t1_row = e(nc.sbuf_tensor("t1_row", [1, T], F32))
        t2_row = e(nc.sbuf_tensor("t2_row", [1, T], F32))
        var_row = e(nc.sbuf_tensor("var_row", [1, T], F32))
        r_row = e(nc.sbuf_tensor("r_row", [1, T], F32))
        mr_row = e(nc.sbuf_tensor("mr_row", [1, T], F32))
        col_t = e(nc.sbuf_tensor("col_t", [128, 1], F32))
        dB_sl = [(e(nc.sbuf_tensor(f"db{i}", [128, T], F32)), Res()) for i in range(2)]
        ub = e(nc.sbuf_tensor("ub", [128, 128], F32))
        cb = e(nc.sbuf_tensor("cb", [128, 128], F32))
        sb_red = e(nc.sbuf_tensor("sb_red", [128, 64], F32))
        sb_own = e(nc.sbuf_tensor("sb_own", [128, 64], F32))
        ccs_buf = e(nc.sbuf_tensor("ccs_buf", [128, 64], F32))

        ps_g = [e(nc.psum_tensor(f"psg{i}", [128, T], F32)) for i in range(4)]
        ps_ss = e(nc.psum_tensor("ps_ss", [1, T], F32))
        ps_sq = e(nc.psum_tensor("ps_sq", [1, T], F32))
        ps_br = e(nc.psum_tensor("ps_br", [128, T], F32))
        ps_bm = e(nc.psum_tensor("ps_bm", [128, T], F32))

        sems = {}
        for nm in ("s_ld s_x g_cast s_wat t_mm t_stats t_bc a_ev a_row v_misc v_row v_fin "
                   "v_ap v_uc v_dl v_dA g_sp g_wr g_fl g_po g_rd v_cmb s2_ld w_ld "
                   "a2_ev v2_a g_out dbgst v_acc g_sq").split():
            sems[nm] = e(nc.semaphore(nm))

        r_bank = [Res() for _ in range(4)]
        r_ss, r_sq, r_br, r_bm = Res(), Res(), Res(), Res()
        r_rrow, r_mrrow, r_vrow = Res(), Res(), Res()

        glb = {}

        def emit_ln(B, cast_sl, sq_sl, src_f, src_evs, apply_fn, apply_sig,
                    vsig, asig, cast_eng="vector", src_evs_tile=None):
            stat_last = []
            for f in range(DF):
                cs, cr = cast_sl[f % 2]
                tile_w = [src_evs_tile[f]] if src_evs_tile is not None else []
                if cast_eng == "vector":
                    ev = B.op("vector",
                              (lambda f=f, cs=cs: lambda v: v.tensor_copy(cs[:, :], src_f(f)))(),
                              waits=cr.wr() + tile_w + (list(src_evs) if f == 0 else []), sig=vsig)
                else:
                    ev = B.op("scalar",
                              (lambda f=f, cs=cs: lambda s: s.activation(cs[:, :], src_f(f), AF.Copy))(),
                              waits=cr.wr() + (list(src_evs) if f == 0 else []), sig=asig)
                cr.note_w(ev)
                qs, qr = sq_sl[f % 2]
                ev2 = B.op("scalar",
                           (lambda cs=cs, qs=qs: lambda s: s.activation(qs[:, :], cs[:, :], AF.Square))(),
                           waits=[ev] + qr.wr(), sig=asig)
                cr.note_r(ev2)
                qr.note_w(ev2)
                mm1 = B.op("tensor",
                           (lambda f=f, cs=cs: lambda t: t.matmul(
                               ps_ss[0:1, :], ones_cb[:, 0:1], cs[:, :],
                               start=(f == 0), stop=(f == DF - 1)))(),
                           waits=[ev] + (r_ss.wr() + [glb.get("ones")] if f == 0 else []), sig="t_stats")
                cr.note_r(mm1)
                mm2 = B.op("tensor",
                           (lambda f=f, qs=qs: lambda t: t.matmul(
                               ps_sq[0:1, :], ones_cb[:, 0:1], qs[:, :],
                               start=(f == 0), stop=(f == DF - 1)))(),
                           waits=[ev2] + (r_sq.wr() if f == 0 else []), sig="t_stats")
                qr.note_r(mm2)
                if f == DF - 1:
                    stat_last = [mm1, mm2]
            ev = B.op("vector", lambda v: v.tensor_scalar(t1_row[:, :], ps_ss[0:1, :], INVD, None, ALU.mult),
                      waits=stat_last + r_vrow.wr(), sig=vsig)
            r_ss.note_r(ev)
            B.op("vector", lambda v: v.tensor_mul(t2_row[:, :], t1_row[:, :], t1_row[:, :]))
            evq = B.op("vector", lambda v: v.tensor_scalar(var_row[:, :], ps_sq[0:1, :], INVD, EPS,
                                                           ALU.mult, ALU.add), sig=vsig)
            r_sq.note_r(evq)
            ev_v = B.op("vector", lambda v: v.tensor_sub(var_row[:, :], var_row[:, :], t2_row[:, :]),
                        sig="v_row")
            r_vrow.note_w(ev_v)
            ev_l = B.op("scalar", lambda s: s.activation(r_row[:, :], var_row[:, :], AF.Ln),
                        waits=[ev_v] + r_rrow.wr(), sig=asig)
            r_vrow.note_r(ev_l)
            ev_r = B.op("scalar", lambda s: s.activation(r_row[:, :], r_row[:, :], AF.Exp, scale=-0.5),
                        sig="a_row")
            r_rrow.note_w(ev_r)
            ev_m = B.op("vector", lambda v: v.tensor_mul(mr_row[:, :], t1_row[:, :], r_row[:, :]),
                        waits=[ev_r] + r_mrrow.wr(), sig="v_fin")
            r_rrow.note_r(ev_m)
            r_mrrow.note_w(ev_m)
            bc1 = B.op("tensor", lambda t: t.matmul(ps_br[:, :], ones_rf[0:1, :], r_row[0:1, :],
                                                    start=True, stop=True),
                       waits=[ev_m, glb.get("ones")] + r_br.wr(), sig="t_bc")
            r_rrow.note_r(bc1)
            r_br.note_w(bc1)
            bc2 = B.op("tensor", lambda t: t.matmul(ps_bm[:, :], ones_rf[0:1, :], mr_row[0:1, :],
                                                    start=True, stop=True),
                       waits=r_bm.wr(), sig="t_bc")
            r_mrrow.note_r(bc2)
            r_bm.note_w(bc2)
            last = None
            for f in range(DF):
                last = apply_fn(f, [bc1, bc2] if f == 0 else [],
                                apply_sig if f == DF - 1 else None)
            r_br.note_r(last)
            r_bm.note_r(last)
            return last

        B = Sched()
        with ExitStack() as p1:
            e1 = p1.enter_context
            xb = [e1(nc.sbuf_tensor(f"xb{i}", [128, DF * T], F32)) for i in range(2)]
            r_xb = [Res(), Res()]
            cast_sl = [(e1(nc.sbuf_tensor(f"cb{i}", [128, T], BF16)), Res()) for i in range(2)]
            sq_sl = [(e1(nc.sbuf_tensor(f"qb{i}", [128, T], BF16)), Res()) for i in range(2)]
            xtmp = [e1(nc.sbuf_tensor(f"xt{i}", [128, T], F32)) for i in range(2)]
            xnbf = e1(nc.sbuf_tensor("xnbf", [128, DF * T], BF16))
            dpre = e1(nc.sbuf_tensor("dpre", [128, DF * T], BF16))
            cs_sl = [(e1(nc.sbuf_tensor(f"csl{i}", [128, T], F32)), Res()) for i in range(2)]
            e_sl = [(e1(nc.sbuf_tensor(f"esl{i}", [128, T], F32)), Res()) for i in range(2)]
            eA_sl = [(e1(nc.sbuf_tensor(f"easl{i}", [128, T], F32)), Res()) for i in range(2)]
            s_sl = [e1(nc.sbuf_tensor(f"ssl{i}", [128, T], F32)) for i in range(2)]
            delta_sl = [(e1(nc.sbuf_tensor(f"dl{i}", [128, T], F32)), Res()) for i in range(2)]
            dA_sl = [(e1(nc.sbuf_tensor(f"da{i}", [128, T], F32)), Res()) for i in range(2)]
            res_sl = [(e1(nc.sbuf_tensor(f"re{i}", [128, T], F32)), Res()) for i in range(2)]
            junk = e1(nc.sbuf_tensor("junk", [128, T], F32))
            watn = e1(nc.sbuf_tensor("watn", [128, 4 * DF * DF * 128], BF16))

            r_xnbf, r_dpre = Res(), Res()

            def watile(m, k):
                o = (m * DF + k) * 128
                return watn[:, o:o + 128]

            ev_vec = B.op("sync", lambda s: s.dma_start(out=vecs[:, :], in_=vecs_in), sig="s_ld", dk=16)
            B.op("sync", lambda s: s.dma_start(out=meta_s[:, :], in_=meta_in), sig="s_ld", dk=16)
            B.op("vector", lambda v: v.memset(ones_cb[:, :], 1.0), sig="v_misc")
            B.op("vector", lambda v: v.memset(ones_rf[:, :], 1.0), sig="v_misc")
            B.op("vector", lambda v: v.memset(acc[:, :], 0.0), sig="v_misc")
            B.op("vector", lambda v: v.memset(ub[:, :], 0.0), sig="v_misc")
            B.op("vector", lambda v: v.memset(cb[:, :], 0.0), sig="v_misc")
            B.op("vector", lambda v: v.memset(sb_red[:, :], 0.0), sig="v_misc")
            ev_init = B.op("vector", lambda v: v.memset(sb_own[:, :], 0.0), sig="v_misc")
            glb["ones"] = ev_init

            cast_evs = {}

            def emit_cast_piece(nm, mlo, mhi):
                srcw = dict(zip(wb_names, wb_src))[nm]
                cast_evs[nm] = B.op("gpsimd",
                                    (lambda srcw=srcw, dst=wb[nm], mlo=mlo, mhi=mhi:
                                     lambda g: g.dma_start(out=dst[mlo:mhi], in_=srcw[mlo:mhi]))(),
                                    sig="g_cast", dk=16)

            emit_cast_piece("wproj", 0, 3 * DF)
            emit_cast_piece("wdelta", 0, DF)
            cast_pieces = ([("w1", 16 * i, 16 * i + 16) for i in range(2)]
                           + [("w2", 16 * i, 16 * i + 16) for i in range(2)]
                           + [("h1", 4 * i, 4 * i + 4) for i in range(8)]
                           + [("h2", 4 * i, 4 * i + 4) for i in range(8)]
                           + [("w3", 0, DF)])
            ev_wat1 = B.op("scalar", lambda s: s.dma_start(
                out=watn.ap()[:, 0:3 * DF * DF * 128].rearrange("p (m k j) -> p m k j", k=DF, j=128),
                in_=wb["wproj"].rearrange("m p k j -> p m k j")),
                waits=[cast_evs["wproj"]], sig="s_wat", dk=16)
            ev_wat2 = B.op("scalar", lambda s: s.dma_start(
                out=watn.ap()[:, 3 * DF * DF * 128:].rearrange("p (m k j) -> p m k j", k=DF, j=128),
                in_=wb["wdelta"].rearrange("m p k j -> p m k j")),
                waits=[cast_evs["wdelta"]], sig="s_wat", dk=16)

            sched = [("own", c) for c in range(NCH)] + [("red", c) for c in range(NCH)]
            for g, (kind, c) in enumerate(sched):
                for _ in range(2):
                    if cast_pieces:
                        emit_cast_piece(*cast_pieces.pop(0))
                src_x = x_own if kind == "own" else x_red
                xcur, rx = xb[g % 2], r_xb[g % 2]
                ev_x = B.op("sync", (lambda src_x=src_x, c=c, xcur=xcur: lambda s: s.dma_start(
                    out=sb3(xcur), in_=dram_fm(src_x, c)))(),
                    waits=rx.wr(), sig="s_x", dk=16)
                rx.note_w(ev_x)

                def apply_ln1(f, w0, sig, xcur=xcur, rx=rx):
                    B.op("vector", (lambda f=f, xcur=xcur: lambda v: v.tensor_mul(
                        xtmp[f % 2][:, :], xcur[:, f * T:(f + 1) * T], ps_br[:, :]))(),
                        waits=w0 + (r_xnbf.wr() if f == 0 else []))
                    ev2 = B.op("vector", (lambda f=f: lambda v: v.tensor_sub(
                        xnbf[:, f * T:(f + 1) * T], xtmp[f % 2][:, :], ps_bm[:, :]))(), sig=sig)
                    return ev2

                ev_ap = emit_ln(B, cast_sl, sq_sl,
                                (lambda xcur=xcur: lambda f: xcur[:, f * T:(f + 1) * T])(),
                                [ev_x], apply_ln1, "v_ap", "v_misc", "a_ev",
                                cast_eng="scalar")
                rx.note_r(ev_ap)
                r_xnbf.note_w(ev_ap)

                n_m = 24 if kind == "own" else 16
                ev_dpre_done = None
                ev_res_done = None
                for m in range(n_m):
                    bank, rb = ps_g[m % 2], r_bank[m % 2]
                    mm = None
                    for k in range(DF):
                        mm = B.op("tensor", (lambda m=m, k=k, bank=bank: lambda t: t.matmul(
                            bank[:, :], watile(m, k), xnbf[:, k * T:(k + 1) * T],
                            start=(k == 0), stop=(k == DF - 1)))(),
                            waits=(([ev_ap, ev_wat1, ev_wat2] + rb.wr()) if k == 0 else []),
                            sig=("t_mm" if k == DF - 1 else None))
                    rb.note_w(mm)
                    r_xnbf.note_r(mm)
                    if m < 8:
                        evv = B.op("scalar", (lambda m=m, bank=bank: lambda s: s.activation(
                            dpre[:, m * T:(m + 1) * T], bank[:, :], AF.Copy))(),
                            waits=[mm] + (r_dpre.wr() if m == 0 else []), sig="a_ev")
                        r_dpre.note_w(evv)
                        rb.note_r(evv)
                        if m == 7:
                            ev_dpre_done = evv
                    elif m < 16:
                        cs, cr = cs_sl[m % 2]
                        evv = B.op("scalar", (lambda cs=cs, bank=bank: lambda s: s.activation(
                            cs[:, :], bank[:, :], AF.Copy))(),
                            waits=[mm] + cr.wr(), sig="a_ev")
                        cr.note_w(evv)
                        rb.note_r(evv)
                        base = 0 if kind == "own" else 64
                        f = m - 8
                        pc = base + c * 8 + f
                        B.op("vector", (lambda cs=cs, f=f, pc=pc: lambda v: v.scalar_tensor_tensor(
                            junk[:, :], cs[:, :], 1.0, xnbf[:, f * T:(f + 1) * T],
                            ALU.bypass, ALU.mult, accum_out=ub[:, pc:pc + 1]))(),
                            waits=[evv])
                        e_u2 = B.op("vector", (lambda cs=cs, pc=pc: lambda v: v.reduce_sum(
                            cb[:, pc:pc + 1], cs[:, :], axis=AX.X))(), sig="v_uc")
                        cr.note_r(e_u2)
                    else:
                        f = m - 16
                        rs, rr = res_sl[f % 2]
                        evv = B.op("scalar", (lambda rs=rs, bank=bank: lambda s: s.activation(
                            rs[:, :], bank[:, :], AF.Copy))(),
                            waits=[mm] + rr.wr(), sig="a_ev")
                        rr.note_w(evv)
                        rb.note_r(evv)
                        ev_sp = B.op("gpsimd", (lambda rs=rs, f=f, c=c: lambda g_: g_.dma_start(
                            out=sp_res[f * 128:(f + 1) * 128, c * T:(c + 1) * T], in_=rs[:, :]))(),
                            waits=[evv], sig="g_sp", dk=16)
                        rr.note_r(ev_sp)

                ev_sc = None
                ev_da = None
                for m in range(DF):
                    bank, rb = ps_g[m % 2], r_bank[m % 2]
                    mm = None
                    for k in range(DF):
                        mm = B.op("tensor", (lambda m=m, k=k, bank=bank: lambda t: t.matmul(
                            bank[:, :], watile(24 + m, k), dpre[:, k * T:(k + 1) * T],
                            start=(k == 0), stop=(k == DF - 1)))(),
                            waits=(([ev_dpre_done] + rb.wr()) if k == 0 else []),
                            sig=("t_mm" if k == DF - 1 else None))
                    rb.note_w(mm)
                    r_dpre.note_r(mm)
                    es_, er_ = e_sl[m % 2]
                    ev_u = B.op("scalar", (lambda es_=es_, bank=bank, m=m: lambda s: s.activation(
                        es_[:, :], bank[:, :], AF.Exp,
                        bias=vecs[:, VC_BDELTA + m:VC_BDELTA + m + 1]))(),
                        waits=[mm, ev_vec] + er_.wr(), sig="a_ev")
                    er_.note_w(ev_u)
                    rb.note_r(ev_u)
                    ds, dr = delta_sl[m % 2]
                    ev_dl = B.op("scalar", (lambda es_=es_, ds=ds, m=m: lambda s: s.activation(
                        ds[:, :], es_[:, :], AF.Ln, bias=1.0))(),
                        waits=dr.wr(), sig="v_dl")
                    er_.note_r(ev_dl)
                    dr.note_w(ev_dl)
                    ev_s = B.op("vector", (lambda ds=ds, m=m: lambda v: v.tensor_scalar_mul(
                        s_sl[m % 2][:, :], ds[:, :],
                        vecs[:, VC_B1 + m:VC_B1 + m + 1]))(), waits=[ev_vec, ev_dl], sig="v_misc")
                    dr.note_r(ev_s)
                    if kind == "own":
                        bs, br_ = dB_sl[m % 2]
                        ev_sc = B.op("vector", (lambda bs=bs, m=m: lambda v: v.tensor_tensor_scan(
                            bs[:, :], s_sl[m % 2][:, :], s_sl[m % 2][:, :],
                            0.0, ALU.add, ALU.bypass))(),
                            waits=br_.wr(), sig="v_misc")
                        br_.note_w(ev_sc)
                        pc = c * 8 + m
                        B.op("vector", (lambda m=m, pc=pc: lambda v: v.reduce_sum(
                            sb_own[:, pc:pc + 1], s_sl[m % 2][:, :], axis=AX.X))())
                        ev_spb = B.op("gpsimd", (lambda bs=bs, m=m, c=c: lambda g_: g_.dma_start(
                            out=sp_dB[m * 128:(m + 1) * 128, c * T:(c + 1) * T], in_=bs[:, :]))(),
                            waits=[ev_sc], sig="g_sp", dk=16)
                        br_.note_r(ev_spb)
                        es, er = eA_sl[m % 2]
                        ev_e = B.op("scalar", (lambda es=es, ds=ds, m=m: lambda s: s.activation(
                            es[:, :], ds[:, :], AF.Exp,
                            scale=vecs[:, VC_A1 + m:VC_A1 + m + 1]))(),
                            waits=[ev_dl] + er.wr(), sig="a_ev")
                        er.note_w(ev_e)
                        dr.note_r(ev_e)
                        as_, ar_ = dA_sl[m % 2]
                        ev_da = B.op("vector", (lambda es=es, as_=as_, m=m: lambda v: v.tensor_scalar_mul(
                            as_[:, :], es[:, :],
                            vecs[:, VC_B1 + m:VC_B1 + m + 1]))(),
                            waits=[ev_e] + ar_.wr(), sig="v_dA")
                        er.note_r(ev_da)
                        ar_.note_w(ev_da)
                        ev_spa = B.op("gpsimd", (lambda as_=as_, m=m, c=c: lambda g_: g_.dma_start(
                            out=sp_dA[m * 128:(m + 1) * 128, c * T:(c + 1) * T], in_=as_[:, :]))(),
                            waits=[ev_da], sig="g_sp", dk=16)
                        ar_.note_r(ev_spa)
                    else:
                        pc = c * 8 + m
                        B.op("vector", (lambda m=m, pc=pc: lambda v: v.reduce_sum(
                            sb_red[:, pc:pc + 1], s_sl[m % 2][:, :], axis=AX.X))())


            while cast_pieces:
                emit_cast_piece(*cast_pieces.pop(0))

            B.check()
            with nc.Block("A") as blkA:
                blkA.sync(lambda h: B.replay("sync", h, sems))
                blkA.tensor(lambda h: B.replay("tensor", h, sems))
                blkA.vector(lambda h: B.replay("vector", h, sems))
                blkA.scalar(lambda h: B.replay("scalar", h, sems))
                blkA.gpsimd(lambda h: B.replay("gpsimd", h, sems))

            def _bva(v):
                tgts = []
                for m in range(DF):
                    tgts.append((AC_U_OWN + m, ub, m))
                    tgts.append((AC_C_OWN + m, cb, m))
                    tgts.append((AC_U_RED + m, ub, 64 + m))
                    tgts.append((AC_C_RED + m, cb, 64 + m))
                    tgts.append((AC_S_RED + m, sb_red, m))
                    tgts.append((AC_S_OWN + m, sb_own, m))
                for col, srcb, base in tgts:
                    lo = base - base % 64
                    view = srcb.ap()[:, lo:lo + 64].rearrange("p (c m) -> p m c", m=8)
                    v.reduce_sum(acc[:, col:col + 1], view[:, base % 8:base % 8 + 1, :],
                                 axis=AX.X)
                for _ in range(3):
                    v.nop(cycle_cnt=250)
                v.engine_nop().then_inc(sems["v_acc"], 1)
                _bv(v)

            def _bg(g_):
                g_.wait_ge(sems["v_acc"], 1)
                with g_.register("pos") as pos, g_.register("nonce") as nonce:
                    g_.load(pos, meta_s[0:1, 0:1])
                    g_.load(nonce, meta_s[0:1, 1:2])
                    g_.memset(flg_s[:, :], 0)
                    g_.store(flg_s[0:1, 0:1], nonce)
                    for p in range(2):
                        with g_.If_eq(pos, p):
                            g_.dma_start(out=xchg[p, :, :], in_=acc[:, :]).then_inc(sems["g_wr"], 16)
                    g_.wait_ge(sems["g_wr"], 16)
                    for p in range(2):
                        with g_.If_eq(pos, p):
                            g_.dma_start(out=flg[p:p + 1, :], in_=flg_s[0:1, :]).then_inc(sems["g_fl"], 16)
                    g_.wait_ge(sems["g_fl"], 16)
                    with g_.register("cond") as cond, g_.register("cnt") as cnt, g_.register("fv") as fv:
                        g_.reg_mov(cond, 1)
                        g_.reg_mov(cnt, 16)
                        with g_.While(cond):
                            for p in range(2):
                                with g_.If_eq(pos, p):
                                    g_.dma_start(out=flg_r[:, :], in_=flg[1 - p:2 - p, :]).then_inc(sems["g_po"], 16)
                            g_.wait_ge(sems["g_po"], cnt)
                            g_.reg_add(cnt, cnt, 16)
                            g_.load(fv, flg_r[0:1, 0:1])
                            g_.reg_alu(cond, fv, nonce, ALU.not_equal)
                    for p in range(2):
                        with g_.If_eq(pos, p):
                            g_.dma_start(out=pacc[:, :], in_=xchg[1 - p, :, :]).then_inc(sems["g_rd"], 16)
                    g_.wait_ge(sems["g_rd"], 16)

            def _bv(v):
                v.wait_ge(sems["g_rd"], 16)

                def space():
                    for _ in range(3):
                        v.nop(cycle_cnt=250)
                v.tensor_add(ucc[:, 0:8], acc[:, AC_U_OWN:AC_U_OWN + 8], acc[:, AC_U_RED:AC_U_RED + 8])
                v.tensor_add(ucc[:, 8:16], acc[:, AC_C_OWN:AC_C_OWN + 8], acc[:, AC_C_RED:AC_C_RED + 8])
                v.tensor_scalar_mul(ucc[:, 24:32], acc[:, AC_S_RED:AC_S_RED + 8], vecs[:, VC_CA:VC_CA + 1])
                space()
                v.tensor_add(ucc[:, 0:8], ucc[:, 0:8], pacc[:, AC_U_OWN:AC_U_OWN + 8])
                v.tensor_add(ucc[:, 8:16], ucc[:, 8:16], pacc[:, AC_C_OWN:AC_C_OWN + 8])
                v.scalar_tensor_tensor(ucc[:, 24:32], pacc[:, AC_S_OWN:AC_S_OWN + 8],
                                       vecs[:, VC_CB:VC_CB + 1], ucc[:, 24:32], ALU.mult, ALU.add)
                space()
                v.tensor_add(ucc[:, 0:8], ucc[:, 0:8], pacc[:, AC_U_RED:AC_U_RED + 8])
                v.tensor_add(ucc[:, 8:16], ucc[:, 8:16], pacc[:, AC_C_RED:AC_C_RED + 8])
                v.scalar_tensor_tensor(ucc[:, 24:32], pacc[:, AC_S_RED:AC_S_RED + 8],
                                       vecs[:, VC_CC:VC_CC + 1], ucc[:, 24:32], ALU.mult, ALU.add)
                space()
                for c_ in range(NCH):
                    v.tensor_mul(ccs_buf[:, c_ * 8:(c_ + 1) * 8], ucc[:, 24:32], ucc[:, 8:16])
                    v.tensor_add(ucc[:, 24:32], ucc[:, 24:32], sb_own[:, c_ * 8:(c_ + 1) * 8])
                    space()
                v.engine_nop().then_inc(sems["v_cmb"], 1)


        B2 = Sched(counts=B.counts)
        with ExitStack() as p3:
            e3 = p3.enter_context
            R1 = e3(nc.sbuf_tensor("R1", [128, DF * T], F32))
            R2 = [e3(nc.sbuf_tensor(f"R2_{i}", [128, DF * T], F32)) for i in range(2)]
            R3 = [e3(nc.sbuf_tensor(f"R3_{i}", [128, DF * T], F32)) for i in range(2)]
            R4 = [e3(nc.sbuf_tensor(f"R4_{i}", [128, DF * T], F32)) for i in range(2)]
            R5 = e3(nc.sbuf_tensor("R5", [128, DF * T], F32))
            R6 = e3(nc.sbuf_tensor("R6", [128, DF * T], BF16))
            cast2 = [(e3(nc.sbuf_tensor(f"c2_{i}", [128, T], BF16)), Res()) for i in range(2)]
            sq2 = [(e3(nc.sbuf_tensor(f"q2_{i}", [128, T], BF16)), Res()) for i in range(2)]
            xtmp2 = [e3(nc.sbuf_tensor(f"xt2_{i}", [128, T], F32)) for i in range(2)]
            gsl = [(e3(nc.sbuf_tensor(f"gg{i}", [128, T], F32)), Res()) for i in range(2)]
            wbuf = e3(nc.sbuf_tensor("wbuf", [128, 4 * HF * 128], BF16))
            r_wslot = [Res() for _ in range(4)]
            r_R1, r_R5, r_R6 = Res(), Res(), Res()
            r_R2 = [Res(), Res()]
            r_R3 = [Res(), Res()]
            r_R4 = [Res(), Res()]

            def wtile(s, k):
                o = s * HF * 128 + k * 128
                return wbuf[:, o:o + 128]

            R1b = R1.ap().bitcast(BF16)
            R2b = [R2[i].ap().bitcast(BF16) for i in range(2)]
            R3b = [R3[i].ap().bitcast(BF16) for i in range(2)]
            R4b = [R4[i].ap().bitcast(BF16) for i in range(2)]

            def a_tile(m, p):
                bb = R2b[p] if m < 16 else R3b[p]
                return bb[:, (m % 16) * T:((m % 16) + 1) * T]

            def b_tile(m, p):
                bb = R4b[p] if m < 16 else R1b
                return bb[:, (m % 16) * T:((m % 16) + 1) * T]

            c_tile = a_tile
            slot_i = 0

            def _bva_wrap(v):
                _bva(v)
                return None
            B2.counts["v_cmb"] = B2.counts.get("v_cmb", 0) + 1
            ev_cmb = Ev("v_cmb", B2.counts["v_cmb"], "vector", 1)
            B2.q["vector"].append(([], _bva_wrap, None, 1))
            B2.watermark["vector"]["v_cmb"] = B2.counts["v_cmb"]
            B2.q["gpsimd"].append(([], lambda g_: _bg(g_), None, 1))
            if debug:
                def _dbg_store(sy):
                    sy.wait_ge(sems["v_cmb"], 1)
                    sy.dma_start(out=dbg["acc"], in_=acc[:, :]).then_inc(sems["dbgst"], 16)
                    sy.dma_start(out=dbg["ucc"], in_=ucc[:, :]).then_inc(sems["dbgst"], 16)
                    sy.wait_ge(sems["dbgst"], 32)
                B2.q["sync"].append(([], _dbg_store, None, 1))

            def stream2(arr, kf, mlo, mhi, cast_ev=None):
                nonlocal slot_i
                s = slot_i % 4
                slot_i += 1
                ncols = (mhi - mlo) * kf * 128
                ev = B2.op("sync", (lambda arr=arr, mlo=mlo, mhi=mhi, s=s, ncols=ncols, kf=kf: lambda sy: sy.dma_start(
                    out=wbuf.ap()[:, s * HF * 128: s * HF * 128 + ncols].rearrange(
                        "p (m k j) -> p m k j", k=kf, j=128),
                    in_=arr[mlo:mhi].rearrange("m p k j -> p m k j")))(),
                    waits=r_wslot[s].wr() + [cast_ev], sig="w_ld", dk=16)
                r_wslot[s].note_w(ev)
                return ev, s

            def prologue(c):
                p = c % 2
                lv = {}
                for nm, R, rr, src in (("R2", R2[p], r_R2[p], sp_dA),
                                       ("R3", R3[p], r_R3[p], sp_dB),
                                       ("R4", R4[p], r_R4[p], sp_res)):
                    lv[nm] = B2.op("sync", (lambda R=R, src=src, c=c: lambda s: s.dma_start(
                        out=sb3(R), in_=dram_fm(src, c)))(),
                        waits=rr.wr(), sig="s2_ld", dk=16)
                    rr.note_w(lv[nm])
                ev_y = None
                for f in range(DF):
                    B2.op("vector", (lambda f=f, c=c, p=p: lambda v: v.tensor_scalar(
                        R4[p][:, f * T:(f + 1) * T], R4[p][:, f * T:(f + 1) * T],
                        vecs[:, VC_D:VC_D + 1], ccs_buf[:, c * 8 + f:c * 8 + f + 1],
                        ALU.mult, ALU.add))(),
                        waits=([lv["R4"], ev_cmb] if f == 0 else []))
                    B2.op("vector", (lambda f=f, p=p: lambda v: v.scalar_tensor_tensor(
                        R4[p][:, f * T:(f + 1) * T], R3[p][:, f * T:(f + 1) * T],
                        ucc[:, 8 + f:8 + f + 1], R4[p][:, f * T:(f + 1) * T], ALU.mult, ALU.add))(),
                        waits=[lv["R3"]] if f == 0 else [])
                    ev_y = B2.op("vector", (lambda f=f, p=p: lambda v: v.scalar_tensor_tensor(
                        R4[p][:, f * T:(f + 1) * T], R2[p][:, f * T:(f + 1) * T],
                        ucc[:, 0 + f:0 + f + 1], R4[p][:, f * T:(f + 1) * T], ALU.mult, ALU.add))(),
                        waits=[lv["R2"]] if f == 0 else [],
                        sig=("v2_a" if f == DF - 1 else None))
                r_R2[p].note_r(ev_y)
                r_R3[p].note_r(ev_y)
                r_R4[p].note_w(ev_y)
                return ev_y

            pro_y = prologue(0)
            for c in range(NCH):
                p = c % 2
                ev_y = pro_y
                lev_x = B2.op("sync", (lambda c=c: lambda s: s.dma_start(
                    out=sb3(R1), in_=dram_fm(x_own, c)))(),
                    waits=r_R1.wr(), sig="s2_ld", dk=16)
                r_R1.note_w(lev_x)

                def apply_attn(f, w0, sig):
                    B2.op("vector", (lambda f=f, p=p: lambda v: v.tensor_mul(
                        xtmp2[f % 2][:, :], R4[p][:, f * T:(f + 1) * T], ps_br[:, :]))(), waits=w0)
                    B2.op("vector", (lambda f=f: lambda v: v.tensor_sub(
                        xtmp2[f % 2][:, :], xtmp2[f % 2][:, :], ps_bm[:, :]))())
                    return B2.op("vector", (lambda f=f: lambda v: v.tensor_add(
                        R5[:, f * T:(f + 1) * T], xtmp2[f % 2][:, :], R1[:, f * T:(f + 1) * T]))(),
                        waits=([lev_x] + r_R5.wr()) if f == 0 else [], sig=sig)

                ev_x1 = emit_ln(B2, cast2, sq2,
                                (lambda p=p: lambda f: R4[p][:, f * T:(f + 1) * T])(),
                                [ev_y], apply_attn, "v2_a", "v2_a", "a2_ev")
                r_R4[p].note_r(ev_x1)
                r_R1.note_r(ev_x1)
                r_R5.note_w(ev_x1)

                def apply_ln2(f, w0, sig):
                    B2.op("vector", (lambda f=f: lambda v: v.tensor_mul(
                        xtmp2[f % 2][:, :], R5[:, f * T:(f + 1) * T], ps_br[:, :]))(),
                        waits=w0 + (r_R6.wr() if f == 0 else []))
                    return B2.op("vector", (lambda f=f: lambda v: v.tensor_sub(
                        R6[:, f * T:(f + 1) * T], xtmp2[f % 2][:, :], ps_bm[:, :]))(), sig=sig)

                ev_xn1 = emit_ln(B2, cast2, sq2, lambda f: R5[:, f * T:(f + 1) * T],
                                 [ev_x1], apply_ln2, "v2_a", "v2_a", "a2_ev")
                r_R5.note_r(ev_xn1)
                r_R6.note_w(ev_xn1)

                ev_a_last = None
                for j in range(HF // 4):
                    e_w1, s1 = stream2(wb["w1"], DF, 4 * j, 4 * j + 4, cast_evs["w1"])
                    e_w2, s2 = stream2(wb["w2"], DF, 4 * j, 4 * j + 4, cast_evs["w2"])
                    for mi in range(4):
                        m = 4 * j + mi
                        b1, rb1 = ps_g[m % 2], r_bank[m % 2]
                        b2, rb2 = ps_g[2 + m % 2], r_bank[2 + m % 2]
                        mm1 = mm2 = None
                        for k in range(DF):
                            mm1 = B2.op("tensor", (lambda s1=s1, mi=mi, k=k, b1=b1: lambda t: t.matmul(
                                b1[:, :], wtile(s1, mi * DF + k), R6[:, k * T:(k + 1) * T],
                                start=(k == 0), stop=(k == DF - 1)))(),
                                waits=(([e_w1, ev_xn1] + rb1.wr()) if k == 0 else []),
                                sig=("t_mm" if k == DF - 1 else None))
                        rb1.note_w(mm1)
                        r_wslot[s1].note_r(mm1)
                        r_R6.note_r(mm1)
                        for k in range(DF):
                            mm2 = B2.op("tensor", (lambda s2=s2, mi=mi, k=k, b2=b2: lambda t: t.matmul(
                                b2[:, :], wtile(s2, mi * DF + k), R6[:, k * T:(k + 1) * T],
                                start=(k == 0), stop=(k == DF - 1)))(),
                                waits=(([e_w2] + rb2.wr()) if k == 0 else []),
                                sig=("t_mm" if k == DF - 1 else None))
                        rb2.note_w(mm2)
                        r_wslot[s2].note_r(mm2)
                        r_R6.note_r(mm2)
                        gs, gr = gsl[m % 2]
                        ev_g = B2.op("scalar", (lambda gs=gs, b1=b1, m=m: lambda s: s.activation(
                            gs[:, :], b1[:, :], AF.Silu,
                            scale=vecs[:, VC_W1S + m:VC_W1S + m + 1]))(),
                            waits=[mm1] + gr.wr(), sig="a2_ev")
                        gr.note_w(ev_g)
                        rb1.note_r(ev_g)
                        wx = [mm2, ev_g]
                        if m == 0:
                            wx += r_R2[p].wr()
                        if m == 16:
                            wx += r_R3[p].wr()
                        ev_a = B2.op("vector", (lambda gs=gs, b2=b2, m=m, p=p: lambda v: v.scalar_tensor_tensor(
                            a_tile(m, p), b2[:, :], vecs[:, VC_W2S + m:VC_W2S + m + 1],
                            gs[:, :], ALU.mult, ALU.mult))(),
                            waits=wx, sig="v2_a")
                        gr.note_r(ev_a)
                        rb2.note_r(ev_a)
                        if m == 0:
                            r_R2[p].note_w(ev_a)
                        if m == 16:
                            r_R3[p].note_w(ev_a)
                        ev_a_last = ev_a

                ev_b_last = None
                for m in range(HF):
                    e_w, s = stream2(wb["h1"], HF, m, m + 1, cast_evs["h1"])
                    bank, rb = ps_g[m % 4], r_bank[m % 4]
                    mm = None
                    for k in range(HF):
                        mm = B2.op("tensor", (lambda s=s, k=k, bank=bank, p=p: lambda t: t.matmul(
                            bank[:, :], wtile(s, k), a_tile(k, p),
                            start=(k == 0), stop=(k == HF - 1)))(),
                            waits=(([e_w, ev_a_last] + rb.wr()) if k == 0 else []),
                            sig=("t_mm" if k == HF - 1 else None))
                    rb.note_w(mm)
                    r_wslot[s].note_r(mm)
                    r_R2[p].note_r(mm)
                    r_R3[p].note_r(mm)
                    wx = [mm]
                    if m == 0:
                        wx += r_R4[p].wr()
                    if m == 16:
                        wx += r_R1.wr()
                    ev_b = B2.op("scalar", (lambda m=m, bank=bank, p=p: lambda s: s.activation(
                        b_tile(m, p), bank[:, :], AF.Gelu,
                        scale=vecs[:, VC_H1S + m:VC_H1S + m + 1]))(),
                        waits=wx, sig="a2_ev")
                    rb.note_r(ev_b)
                    if m == 0:
                        r_R4[p].note_w(ev_b)
                    if m == 16:
                        r_R1.note_w(ev_b)
                    ev_b_last = ev_b

                if c + 1 < NCH:
                    pro_y = prologue(c + 1)

                ev_c_last = None
                for m in range(HF):
                    e_w, s = stream2(wb["h2"], HF, m, m + 1, cast_evs["h2"])
                    bank, rb = ps_g[m % 4], r_bank[m % 4]
                    mm = None
                    for k in range(HF):
                        mm = B2.op("tensor", (lambda s=s, k=k, bank=bank, p=p: lambda t: t.matmul(
                            bank[:, :], wtile(s, k), b_tile(k, p),
                            start=(k == 0), stop=(k == HF - 1)))(),
                            waits=(([e_w, ev_b_last] + rb.wr()) if k == 0 else []),
                            sig=("t_mm" if k == HF - 1 else None))
                    rb.note_w(mm)
                    r_wslot[s].note_r(mm)
                    r_R4[p].note_r(mm)
                    r_R1.note_r(mm)
                    wx = [mm]
                    if m == 0:
                        wx += r_R2[p].wr()
                    if m == 16:
                        wx += r_R3[p].wr()
                    ev_c = B2.op("scalar", (lambda m=m, bank=bank, p=p: lambda s: s.activation(
                        c_tile(m, p), bank[:, :], AF.Gelu,
                        scale=vecs[:, VC_H2S + m:VC_H2S + m + 1]))(),
                        waits=wx, sig="a2_ev")
                    rb.note_r(ev_c)
                    if m == 0:
                        r_R2[p].note_w(ev_c)
                    if m == 16:
                        r_R3[p].note_w(ev_c)
                    ev_c_last = ev_c

                ev_f = None
                f_evs = []
                for m in range(DF):
                    e_w, s = stream2(wb["w3"], HF, m, m + 1, cast_evs["w3"])
                    bank, rb = ps_g[m % 4], r_bank[m % 4]
                    mm = None
                    for k in range(HF):
                        mm = B2.op("tensor", (lambda s=s, k=k, bank=bank, p=p: lambda t: t.matmul(
                            bank[:, :], wtile(s, k), c_tile(k, p),
                            start=(k == 0), stop=(k == HF - 1)))(),
                            waits=(([e_w, ev_c_last] + rb.wr()) if k == 0 else []),
                            sig=("t_mm" if k == HF - 1 else None))
                    rb.note_w(mm)
                    r_wslot[s].note_r(mm)
                    r_R2[p].note_r(mm)
                    r_R3[p].note_r(mm)
                    wx = [mm]
                    if m == 0:
                        wx += r_R4[p].wr()
                    ev_f = B2.op("scalar", (lambda m=m, bank=bank, p=p: lambda s: s.activation(
                        R4[p][:, m * T:(m + 1) * T], bank[:, :], AF.Copy,
                        scale=vecs[:, VC_W3S + m:VC_W3S + m + 1]))(),
                        waits=wx, sig="a2_ev")
                    rb.note_r(ev_f)
                    f_evs.append(ev_f)
                    r_R4[p].note_w(ev_f)

                def apply_fl(f, w0, sig):
                    B2.op("vector", (lambda f=f, p=p: lambda v: v.tensor_mul(
                        xtmp2[f % 2][:, :], R4[p][:, f * T:(f + 1) * T], ps_br[:, :]))(),
                        waits=w0 + (r_R1.wr() if f == 0 else []))
                    B2.op("vector", (lambda f=f: lambda v: v.tensor_sub(
                        xtmp2[f % 2][:, :], xtmp2[f % 2][:, :], ps_bm[:, :]))())
                    return B2.op("vector", (lambda f=f: lambda v: v.tensor_add(
                        R1[:, f * T:(f + 1) * T], xtmp2[f % 2][:, :], R5[:, f * T:(f + 1) * T]))(),
                        sig=sig)

                ev_out = emit_ln(B2, cast2, sq2,
                                 (lambda p=p: lambda f: R4[p][:, f * T:(f + 1) * T])(),
                                 [], apply_fl, "v2_a", "v2_a", "a2_ev",
                                 src_evs_tile=f_evs)
                r_R4[p].note_r(ev_out)
                r_R5.note_r(ev_out)
                r_R1.note_w(ev_out)

                ev_st = B2.op("sync", (lambda c=c: lambda sy: sy.dma_start(
                    out=dram_fm(out_t, c), in_=sb3(R1)))(),
                    waits=[ev_out], sig="g_out", dk=16)
                r_R1.note_r(ev_st)

            gout_total = B2.counts.get("g_out", 0)
            B2.op("sync", (lambda gt=gout_total: lambda sy: sy.wait_ge(sems["g_out"], gt))())

            base_counts = dict(B.counts)
            base_counts["v_cmb"] = base_counts.get("v_cmb", 0) + 1
            B2.check(base=base_counts)
            with nc.Block("C") as blkC:
                blkC.sync(lambda h: B2.replay("sync", h, sems))
                blkC.tensor(lambda h: B2.replay("tensor", h, sems))
                blkC.vector(lambda h: B2.replay("vector", h, sems))
                blkC.scalar(lambda h: B2.replay("scalar", h, sems))
                blkC.gpsimd(lambda h: B2.replay("gpsimd", h, sems))

    return nc


def kernel(**inputs):
    global LAST
    x = np.asarray(inputs["L0_attn_x"])
    Wp = np.asarray(inputs["L0_attn_W_proj"])
    A = np.asarray(inputs["L0_attn_A"]).reshape(-1)
    Bp = np.asarray(inputs["L0_attn_Bp"]).reshape(-1)
    Dsc = np.asarray(inputs["L0_attn_D"]).reshape(-1)
    Wd = np.asarray(inputs["L0_attn_W_delta"])
    bd = np.asarray(inputs["L0_attn_b_delta"]).reshape(-1)
    w1w = np.asarray(inputs["L0_ffn_w1_w"]); w1s = np.asarray(inputs["L0_ffn_w1_s"])
    w2w = np.asarray(inputs["L0_ffn_w2_w"]); w2s = np.asarray(inputs["L0_ffn_w2_s"])
    h1w = np.asarray(inputs["L0_ffn_h1_w"]); h1s = np.asarray(inputs["L0_ffn_h1_s"])
    h2w = np.asarray(inputs["L0_ffn_h2_w"]); h2s = np.asarray(inputs["L0_ffn_h2_s"])
    w3w = np.asarray(inputs["L0_ffn_w3_w"]); w3s = np.asarray(inputs["L0_ffn_w3_s"])

    Bsz, L, _ = x.shape
    Q = L // 4
    nc = build_program(Q, debug=_DEBUG)

    wproj_used = np.concatenate([Wp[0:D], Wp[2 * D:3 * D], Wp[3 * D:4 * D]], axis=0)
    tw = {
        "wproj32": _tile_weight(wproj_used), "wdelta32": _tile_weight(Wd),
        "w1_32": _tile_weight(w1w), "w2_32": _tile_weight(w2w),
        "h1_32": _tile_weight(h1w), "h2_32": _tile_weight(h2w),
        "w3_32": _tile_weight(w3w),
    }
    vecs = np.zeros((128, NV), dtype=np.float32)
    vecs[:, VC_BDELTA:VC_BDELTA + DF] = _cols(bd)
    vecs[:, VC_A1:VC_A1 + DF] = _cols(A)
    vecs[:, VC_B1:VC_B1 + DF] = _cols(Bp)
    vecs[:, VC_W1S:VC_W1S + HF] = _cols(w1s)
    vecs[:, VC_W2S:VC_W2S + HF] = _cols(w2s)
    vecs[:, VC_H1S:VC_H1S + HF] = _cols(h1s)
    vecs[:, VC_H2S:VC_H2S + HF] = _cols(h2s)
    vecs[:, VC_W3S:VC_W3S + DF] = _cols(w3s)
    vecs[:, VC_D] = Dsc[0]

    CARRY_COEF = {0: (0., 0., 0.), 1: (0., 1., 0.), 2: (1., 0., 1.), 3: (1., 1., 1.)}
    RED_Q = {0: 2, 1: 3, 2: 0, 3: 1}

    rng = np.random.default_rng()
    nonces = [int(rng.integers(1, 2**31 - 1)) for _ in range(4)]
    in_maps = []
    for core in range(8):
        b, r = core // 4, core % 4
        xt = np.ascontiguousarray(x[b].T)
        vc = vecs.copy()
        a_, b_, c_ = CARRY_COEF[r]
        vc[:, VC_CA], vc[:, VC_CB], vc[:, VC_CC] = a_, b_, c_
        in_maps.append({
            "x_own": np.ascontiguousarray(xt[:, r * Q:(r + 1) * Q]),
            "x_red": np.ascontiguousarray(xt[:, RED_Q[r] * Q:(RED_Q[r] + 1) * Q]),
            "vecs_in": vc,
            "meta_in": np.array([[core % 2, nonces[core // 2]]], dtype=np.int32),
            **tw,
        })

    res = run_bass_kernel_spmd(nc, in_maps, core_ids=list(range(8)))
    LAST = res

    out = np.empty_like(x)
    for core in range(8):
        b, r = core // 4, core % 4
        out[b, r * Q:(r + 1) * Q, :] = res.results[core]["out"].T
    return out
